# version 06
# speedup vs baseline: 1.5986x; 1.0024x over previous
import math
import numpy as np
import ml_dtypes
import concourse.bass as bass
import concourse.mybir as mybir
from concourse.bass_utils import run_bass_kernel_spmd
from contextlib import ExitStack

F32 = mybir.dt.float32
BF16 = mybir.dt.bfloat16
I32 = mybir.dt.int32
AF = mybir.ActivationFunctionType
ALU = mybir.AluOpType
AX = mybir.AxisListType

D = 2048
TOWN = 1024
TFULL = 2048
NH = 16
HD = 128
KVH = 8
KVD = KVH * HD
LORA_R = 32
EPS_LN = 1e-5
EPS_RMS = float(np.finfo(np.float32).eps)
MASK_NEG = -1e30
BF16NP = ml_dtypes.bfloat16
ENGN = ("sync", "scalar", "vector", "tensor", "gpsimd")


def W(*tokens, **sems):
    d = {}
    for t in tokens:
        if not t:
            continue
        s, v = t
        d[s] = max(d.get(s, 0), v)
    for k, v in sems.items():
        if v:
            d[k] = max(d.get(k, 0), v)
    return d


class Prog:

    def __init__(self, nc, ctx, semnames, queues):
        self.nc = nc
        self.ctx = ctx
        self.steps = {e: [] for e in ENGN}
        self.sh = {n: ctx.enter_context(nc.semaphore(n)) for n in semnames}
        self.sn = {n: 0 for n in semnames}
        self.regs = {}
        self.qs = {}
        for q, r in queues.items():
            sems = []
            for i in range(r):
                n = f"{q}{i}"
                self.sh[n] = ctx.enter_context(nc.semaphore(n))
                self.sn[n] = 0
                sems.append(n)
            self.qs[q] = {"sems": sems, "i": 0}

    def inc(self, sem, k=1):
        self.sn[sem] += k
        return self.sn[sem]

    def val(self, sem):
        return self.sn[sem]

    def on(self, eng, fn):
        self.steps[eng].append(fn)

    def wait(self, eng, waits):
        items = [(self.sh[n], v) for n, v in waits.items() if v > 0]

        def f(e):
            for h, v in items:
                e.wait_ge(h, v)
        self.on(eng, f)

    def dma(self, eng, q, out, in_, waits=None, **kw):
        st = self.qs[q]
        sem = st["sems"][st["i"] % len(st["sems"])]
        st["i"] += 1
        h = self.sh[sem]
        gate = self.sn[sem]
        wl = [(self.sh[n], v) for n, v in (waits or {}).items() if v > 0]

        def f(e):
            if gate:
                e.wait_ge(h, gate)
            for hh, v in wl:
                e.wait_ge(hh, v)
            e.dma_start(out=out, in_=in_, **kw).then_inc(h, 16)
        self.on(eng, f)
        return (sem, self.inc(sem, 16))

    ESEM = {"tensor": "pe", "vector": "ve", "scalar": "se", "gpsimd": "gp"}

    def chain(self, eng, waits, ops):
        sem = self.ESEM[eng]
        h = self.sh[sem]
        base = self.sn[sem]
        wl = [(self.sh[n], v) for n, v in waits.items() if v > 0]

        def f(e):
            if base:
                e.wait_ge(h, base)
            for hh, v in wl:
                e.wait_ge(hh, v)
            for i, op in enumerate(ops):
                if i:
                    e.wait_ge(h, base + i)
                op(e).then_inc(h, 1)
        self.on(eng, f)
        self.sn[sem] = base + len(ops)
        return self.sn[sem]

    def fence(self):
        F = {}
        for n in ("pe", "ve", "se", "gp", "sg", "fl"):
            if self.sn.get(n):
                F[n] = self.sn[n]
        for q in self.qs:
            for s, v in self.qtokens(q):
                F[s] = max(F.get(s, 0), v)
        for e in ENGN:
            self.wait(e, dict(F))

    def qtokens(self, q):
        st = self.qs[q]
        return [(s, self.sn[s]) for s in st["sems"] if self.sn[s] > 0]

    def build_block(self):
        block = self.ctx.enter_context(self.nc.Block())
        for eng in ENGN:
            steps = self.steps[eng]
            if not steps:
                continue
            getattr(block, eng)(lambda e, steps=steps: [f(e) for f in steps])


SEMALL = ["pe", "ve", "se", "gp", "fl", "plx", "sg"]
QALL = {"q": 8, "a": 6}


_TNAME = [0]


def sb(nc, ctx, n, s, dt=F32):
    _TNAME[0] += 1
    return ctx.enter_context(nc.sbuf_tensor(f"{n}_{_TNAME[0]}", s, dt))


def psb(nc, ctx, n, s, dt=F32):
    _TNAME[0] += 1
    return ctx.enter_context(nc.psum_tensor(f"{n}_{_TNAME[0]}", s, dt))


def emit_rsqrt(P, out_ap, arg_ap, t1, t2, v_pre=0, s_pre=0):
    se = P.chain("scalar", W(ve=v_pre, se=s_pre), [
        lambda a: a.activation(t1, arg_ap, AF.Ln),
        lambda a: a.activation(out_ap, t1, AF.Exp, scale=-0.5),
        lambda a: a.activation(t1, out_ap, AF.Square),
    ])
    ve = P.chain("vector", W(se=se), [
        lambda v: v.tensor_tensor(t2, t1, arg_ap, ALU.mult),
        lambda v: v.tensor_scalar(t2, t2, -0.5, 1.5, ALU.mult, ALU.add),
        lambda v: v.tensor_tensor(out_ap, out_ap, t2, ALU.mult),
    ])
    return se, ve


def emit_parity_setup(P):
    for eng in ("sync", "gpsimd"):
        def f(e, eng=eng):
            par = e.alloc_register(f"par_{eng}")
            P.regs[(eng, "par")] = par
            pid = e.partition_id()
            e.reg_alu(par, pid, 1, ALU.bitwise_and)
            if eng == "gpsimd":
                pc = e.alloc_register("pollcnt")
                P.regs[("gpsimd", "pollcnt")] = pc
                e.reg_mov(pc, 0)
        P.on(eng, f)


def emit_kv_publish(P, flag_idx, copies, FL, PF, Fsh, wait_toks):
    st = P.qs["q"]
    plan = []
    for de, do, src in copies:
        sem = st["sems"][st["i"] % len(st["sems"])]
        st["i"] += 1
        plan.append((de, do, src, sem, P.sn[sem]))
        P.inc(sem, 16)
    wl = [(P.sh[n], v) for n, v in wait_toks.items() if v > 0]

    def fsy(e):
        for h, v in wl:
            e.wait_ge(h, v)
        par = P.regs[("sync", "par")]
        for branch in (1, 0):
            g = e.If_eq(par, 1) if branch == 1 else e.Else()
            with g:
                for de, do, src, sem, gate in plan:
                    if gate:
                        e.wait_ge(P.sh[sem], gate)
                    e.dma_start(out=(do if branch == 1 else de),
                                in_=src).then_inc(P.sh[sem], 16)
        e.end_ifs()
    P.on("sync", fsy)
    copy_toks = [(sem, P.sn[sem]) for _, _, _, sem, _ in plan]

    def fg(e):
        for s, v in copy_toks:
            e.wait_ge(P.sh[s], v)
        par = P.regs[("gpsimd", "par")]
        with e.If_eq(par, 1):
            e.dma_start(out=Fsh.ap()[1:2, flag_idx:flag_idx + 1],
                        in_=FL[0:1, flag_idx:flag_idx + 1]).then_inc(P.sh["sg"], 16)
        with e.Else():
            e.dma_start(out=Fsh.ap()[0:1, flag_idx:flag_idx + 1],
                        in_=FL[0:1, flag_idx:flag_idx + 1]).then_inc(P.sh["sg"], 16)
        e.end_ifs()
        tgt = e.alloc_register(f"tgt{flag_idx}")
        cur = e.alloc_register(f"cur{flag_idx}")
        cond = e.alloc_register(f"cond{flag_idx}")
        pc = P.regs[("gpsimd", "pollcnt")]
        tv = e.value_load(FL[0:1, flag_idx:flag_idx + 1])
        e.reg_mov(tgt, tv)
        e.reg_mov(cond, 1)
        with e.While(cond):
            with e.If_eq(par, 1):
                e.dma_start(out=PF[0:1, 0:1],
                            in_=Fsh.ap()[0:1, flag_idx:flag_idx + 1]).then_inc(P.sh["plx"], 16)
            with e.Else():
                e.dma_start(out=PF[0:1, 0:1],
                            in_=Fsh.ap()[1:2, flag_idx:flag_idx + 1]).then_inc(P.sh["plx"], 16)
            e.end_ifs()
            e.reg_add(pc, pc, 16)
            e.wait_ge(P.sh["plx"], pc)
            cv = e.value_load(PF[0:1, 0:1])
            e.reg_mov(cur, cv)
            e.reg_alu(cond, cur, tgt, ALU.bitwise_xor)
            for _ in range(64):
                e.reg_add(cur, cur, 0)
        e.free_register(tgt); e.free_register(cur); e.free_register(cond)
        e.sem_inc(P.sh["fl"], 1)
    P.on("gpsimd", fg)
    P.inc("sg", 16)
    return P.inc("fl", 1)


def host_consts(seed_nonce):
    c = {}
    c["ident"] = np.eye(128, dtype=np.float32)
    c["ones128"] = np.ones((128, 128), dtype=np.float32)
    c["flagvals"] = (int(seed_nonce) + 1 + np.arange(8, dtype=np.int64)).astype(np.int32)[None, :]
    return c


def rope_tables_np():
    inv = 1.0 / (10000.0 ** (np.arange(0, 64, 2, dtype=np.float32) / 64.0))
    t = np.arange(TFULL, dtype=np.float32)
    fr = np.outer(inv.astype(np.float64), t.astype(np.float64))
    return np.cos(fr).astype(np.float32), np.sin(fr).astype(np.float32)


def causal_diag_masks_np():
    out = np.zeros((4, 128, 512), dtype=np.float32)
    for i, o in enumerate([0, 128, 256, 384]):
        q = np.arange(128)[:, None]
        j = np.arange(512)[None, :]
        out[i] = np.where(j <= o + q, 0.0, MASK_NEG)
    return out


def emit_ln_split(P, A, xsrc, XH, XL, with_rms2):
    ONES, ps_a, ps_b = A["ONES"], A["ps_a"], A["ps_b"]
    ST = A["ST"]
    prev_stats_ve = 0
    for tg in range(2):
        pe_mark = {}
        prev_apply_ve = P.val("ve")
        prev_apply_se = P.val("se")
        for dc in range(16):
            ld = P.dma("sync", "q", A["XB"][dc % 2][:, :],
                       xsrc[dc * 128:(dc + 1) * 128, tg * 512:(tg + 1) * 512],
                       waits=W(pe=pe_mark.get(dc - 2, 0),
                               ve=prev_apply_ve if dc < 2 else 0,
                               se=prev_apply_se if dc < 2 else 0))
            se = P.chain("scalar", W(*[ld], pe=pe_mark.get(dc - 2, 0)), [
                lambda a, dc=dc: a.activation(A["X2"][dc % 2][:, :],
                                              A["XB"][dc % 2][:, :], AF.Square)])
            pe_mark[dc] = P.chain(
                "tensor", W(*[ld], se=se,
                            ve=prev_stats_ve if dc == 0 else 0), [
                    lambda t, dc=dc: t.matmul(ps_a[:, :], ONES[:, :],
                                              A["XB"][dc % 2][:, :],
                                              start=(dc == 0), stop=(dc == 15)),
                    lambda t, dc=dc: t.matmul(ps_b[:, :], ONES[:, :],
                                              A["X2"][dc % 2][:, :],
                                              start=(dc == 0), stop=(dc == 15))])
        pe_done = P.val("pe")
        MB, RB = A["MB"][0], A["RB"][0]
        ve = P.chain("vector", W(pe=pe_done), [
            lambda v, MB=MB: v.tensor_scalar(MB[:, :], ps_a[:, :], 1.0 / D, None, ALU.mult),
            lambda v: v.tensor_scalar(ST[0][:, :], ps_b[:, :], 1.0 / D, None, ALU.mult)])
        prev_stats_ve = ve
        se = P.chain("scalar", W(ve=ve), [
            lambda a, MB=MB: a.activation(ST[1][:, :], MB[:, :], AF.Square)])
        ve = P.chain("vector", W(se=se), [
            lambda v: v.tensor_tensor(ST[2][:, :], ST[0][:, :], ST[1][:, :], ALU.subtract),
            lambda v: v.tensor_scalar(ST[0][:, :], ST[2][:, :], EPS_LN, None, ALU.add)])
        se, ve = emit_rsqrt(P, RB[:, :], ST[0][:, :], ST[1][:, :],
                            ST[3][:, :], v_pre=ve)
        if with_rms2:
            ve = P.chain("vector", W(), [
                lambda v, RB=RB: v.tensor_tensor(ST[1][:, :], RB[:, :], RB[:, :], ALU.mult),
                lambda v: v.tensor_tensor(ST[0][:, :], ST[2][:, :], ST[1][:, :], ALU.mult),
                lambda v: v.tensor_scalar(ST[0][:, :], ST[0][:, :], EPS_RMS, None, ALU.add)])
            se, ve = emit_rsqrt(P, ST[2][:, :], ST[0][:, :], ST[1][:, :],
                                ST[3][:, :], v_pre=ve)
            ve = P.chain("vector", W(), [
                lambda v, RB=RB: v.tensor_tensor(RB[:, :], RB[:, :], ST[2][:, :], ALU.mult)])
        stats_ve = P.val("ve")

        ve6_mark = {}
        for dc in range(16):
            ld = P.dma("sync", "q", A["XB"][dc % 2][:, :],
                       xsrc[dc * 128:(dc + 1) * 128, tg * 512:(tg + 1) * 512],
                       waits=W(ve=ve6_mark.get(dc - 2, 0),
                               pe=pe_done if dc < 2 else 0))
            off = dc * TOWN + tg * 512
            ve5 = P.chain("vector", W(*[ld], ve=stats_ve if dc < 2 else 0), [
                lambda v, dc=dc, MB=MB: v.tensor_tensor(
                    A["X2"][dc % 2][:, :], A["XB"][dc % 2][:, :], MB[:, :], ALU.subtract),
                lambda v, dc=dc, RB=RB: v.tensor_tensor(
                    A["X2"][dc % 2][:, :], A["X2"][dc % 2][:, :], RB[:, :], ALU.mult)])
            se5 = P.chain("scalar", W(ve=ve5), [
                lambda a, dc=dc, off=off: a.activation(
                    XH[:, off:off + 512], A["X2"][dc % 2][:, :], AF.Identity)])
            ve6_mark[dc] = P.chain("vector", W(se=se5), [
                lambda v, dc=dc, off=off: v.tensor_tensor(
                    XL[:, off:off + 512], A["X2"][dc % 2][:, :],
                    XH[:, off:off + 512], ALU.subtract)])
    return P.val("ve"), P.val("se")


def emit_strip_load(P, A, Wsrc, f0, fw, gain_tile, sidx, war_pe):
    ld = P.dma("sync", "q",
               A["WS"][:, 0:16 * fw].rearrange("p (c f) -> p c f", c=16),
               Wsrc[:, f0:f0 + fw].rearrange("(c p) f -> p c f", p=128),
               waits=W(pe=war_pe, ve=A.get("_ws_ve", 0)))

    def hop(a, dc):
        kw = {}
        if gain_tile is not None:
            kw["scale"] = gain_tile[:, dc:dc + 1]
        return a.activation(A["WH"][sidx][:, dc * fw:(dc + 1) * fw],
                            A["WS"][:, dc * fw:(dc + 1) * fw], AF.Identity, **kw)

    se = P.chain("scalar", W(*[ld], pe=war_pe),
                 [lambda a, dc=dc: hop(a, dc) for dc in range(16)])

    def lop(v, dc):
        if gain_tile is None:
            return v.tensor_tensor(A["WL"][sidx][:, dc * fw:(dc + 1) * fw],
                                   A["WS"][:, dc * fw:(dc + 1) * fw],
                                   A["WH"][sidx][:, dc * fw:(dc + 1) * fw],
                                   ALU.subtract)
        return v.scalar_tensor_tensor(A["WL"][sidx][:, dc * fw:(dc + 1) * fw],
                                      A["WS"][:, dc * fw:(dc + 1) * fw],
                                      gain_tile[:, dc:dc + 1],
                                      A["WH"][sidx][:, dc * fw:(dc + 1) * fw],
                                      ALU.mult, ALU.subtract)

    ve = P.chain("vector", W(*[ld], se=se),
                 [lambda v, dc=dc: lop(v, dc) for dc in range(16)])
    A["_ws_ve"] = ve
    return se, ve


def emit_mm3(P, ps_ap, whi, wlo, xhi, xlo, waits, extra=None):
    base = P.val("pe")
    wl = dict(waits)

    def ft(t):
        if base:
            t.wait_ge(P.sh["pe"], base)
        for s, v in wl.items():
            if v:
                t.wait_ge(P.sh[s], v)
        ops = []
        for (wa, xa) in ((whi, xhi), (whi, xlo), (wlo, xhi)):
            ops.extend(zip(wa, xa))
        if extra:
            ops.extend(extra)
        ins = None
        for i, (w, x) in enumerate(ops):
            ins = t.matmul(ps_ap, w, x, start=(i == 0), stop=(i == len(ops) - 1))
        ins.then_inc(P.sh["pe"], 1)
    P.on("tensor", ft)
    return P.inc("pe")


def declare_block0a_io(nc, pfx="", shared=None, xT=None, out=None):
    io = {}
    io["xT"] = xT if xT is not None else nc.dram_tensor(pfx + "xT", [D, TOWN], F32, kind="ExternalInput").ap()
    io["Wq"] = nc.dram_tensor(pfx + "Wq", [D, D], F32, kind="ExternalInput").ap()
    io["bq"] = nc.dram_tensor(pfx + "bq", [D], F32, kind="ExternalInput").ap()
    io["Wkv"] = nc.dram_tensor(pfx + "Wkv", [D, 2 * D], F32, kind="ExternalInput").ap()
    io["bkv"] = nc.dram_tensor(pfx + "bkv", [2 * D], F32, kind="ExternalInput").ap()
    io["Wo"] = nc.dram_tensor(pfx + "Wo", [D, D], F32, kind="ExternalInput").ap()
    io["bo"] = nc.dram_tensor(pfx + "bo", [D], F32, kind="ExternalInput").ap()
    io["g_q"] = nc.dram_tensor(pfx + "g_q", [D], F32, kind="ExternalInput").ap()
    io["g_kv"] = nc.dram_tensor(pfx + "g_kv", [D], F32, kind="ExternalInput").ap()
    io["g_head"] = nc.dram_tensor(pfx + "g_head", [HD], F32, kind="ExternalInput").ap()
    io["ident_d"] = shared["ident_d"] if shared is not None else nc.dram_tensor("ident", [128, 128], F32, kind="ExternalInput").ap()
    io["ones_d"] = shared["ones_d"] if shared is not None else nc.dram_tensor("ones128", [128, 128], F32, kind="ExternalInput").ap()
    io["flags_d"] = shared["flags_d"] if shared is not None else nc.dram_tensor("flagvals", [1, 8], I32, kind="ExternalInput").ap()
    io["x1T"] = out if out is not None else nc.dram_tensor(pfx + "x1T", [D, TOWN], F32, kind="ExternalOutput").ap()
    io["Qsc"] = nc.dram_tensor(pfx + "Qsc", [D, TOWN], F32, kind="Internal")
    io["AVsc"] = nc.dram_tensor(pfx + "AVsc", [D, TOWN], F32, kind="Internal")
    io["Kloc"] = nc.dram_tensor(pfx + "Kloc", [D, TOWN], F32, kind="Internal")
    io["Vloc"] = nc.dram_tensor(pfx + "Vloc", [TOWN, D], F32, kind="Internal")
    io["Ksh"] = nc.dram_tensor(pfx + "Ksh0", [D, TFULL], F32, kind="Internal", addr_space="Shared")
    io["Vsh"] = nc.dram_tensor(pfx + "Vsh0", [TFULL, D], F32, kind="Internal", addr_space="Shared")
    io["Fsh"] = nc.dram_tensor(pfx + "Fsh0", [2, 16], I32, kind="Internal", addr_space="Shared")
    return io


def record_block0a(P, nc, ctx, io):
    xT = io["xT"]; Wq = io["Wq"]; bq = io["bq"]
    Wkv = io["Wkv"]; bkv = io["bkv"]; Wo = io["Wo"]
    bo = io["bo"]; g_q = io["g_q"]; g_kv = io["g_kv"]
    g_head = io["g_head"]; ident_d = io["ident_d"]; ones_d = io["ones_d"]
    flags_d = io["flags_d"]; x1T = io["x1T"]; Qsc = io["Qsc"]
    AVsc = io["AVsc"]; Kloc = io["Kloc"]; Vloc = io["Vloc"]
    Ksh = io["Ksh"]; Vsh = io["Vsh"]; Fsh = io["Fsh"]
    A = {}
    A["XH"] = XH = sb(nc, ctx, "XH", [128, 16 * TOWN], BF16)
    A["XL"] = XL = sb(nc, ctx, "XL", [128, 16 * TOWN], BF16)
    A["WS"] = sb(nc, ctx, "WS", [128, 16 * 256], F32)
    A["WH"] = [sb(nc, ctx, f"WH{i}", [128, 16 * 256], BF16) for i in range(2)]
    A["WL"] = [sb(nc, ctx, f"WL{i}", [128, 16 * 256], BF16) for i in range(2)]
    A["XB"] = [sb(nc, ctx, f"XB{i}", [128, 512], F32) for i in range(2)]
    A["X2"] = [sb(nc, ctx, f"X2{i}", [128, 512], F32) for i in range(2)]
    A["MB"] = [sb(nc, ctx, f"MB{i}", [128, 512], F32) for i in range(1)]
    A["RB"] = [sb(nc, ctx, f"RB{i}", [128, 512], F32) for i in range(1)]
    A["ST"] = [sb(nc, ctx, f"ST{i}", [128, 512], F32) for i in range(4)]
    A["ONES"] = ONES = sb(nc, ctx, "ONES", [128, 128], F32)
    P32 = sb(nc, ctx, "P32", [128, 2048], F32)
    PT = sb(nc, ctx, "PT", [128, 2048], F32)
    KHb = sb(nc, ctx, "KHb", [128, 2048], F32)
    VHb = sb(nc, ctx, "VHb", [128, 2048], F32)
    QHb = sb(nc, ctx, "QHb", [128, TOWN], F32)
    AVH = sb(nc, ctx, "AVH", [128, TOWN], F32)
    AVB = sb(nc, ctx, "AVB", [128, TOWN], F32)
    QT = sb(nc, ctx, "QT", [128, 512], F32)
    Q2T = sb(nc, ctx, "Q2T", [128, 512], F32)
    RS = sb(nc, ctx, "RS", [128, 512], F32)
    RS2 = sb(nc, ctx, "RS2", [128, 512], F32)
    VT = sb(nc, ctx, "VTe", [128, 512], F32)
    IDT = sb(nc, ctx, "IDT", [128, 128], F32)
    BQ = sb(nc, ctx, "BQ", [128, 16], F32)
    BKVk = sb(nc, ctx, "BKVk", [128, 16], F32)
    BO = sb(nc, ctx, "BO", [128, 16], F32)
    BKVrow = sb(nc, ctx, "BKVrow", [1, 4096], F32)
    GQ = sb(nc, ctx, "GQ", [128, 16], F32)
    GKV = sb(nc, ctx, "GKV", [128, 16], F32)
    G2 = sb(nc, ctx, "G2", [128, 8], F32)
    GH = sb(nc, ctx, "GH", [128, 8], F32)
    FL = sb(nc, ctx, "FL", [1, 8], I32)
    PF = sb(nc, ctx, "PF", [1, 1], I32)
    NM = sb(nc, ctx, "NM", [128, 1], F32)
    SM = sb(nc, ctx, "SM", [128, 1], F32)
    ps_s = psb(nc, ctx, "ps_s", [128, 2048])
    ps_a = psb(nc, ctx, "ps_a", [128, 512])
    ps_b = psb(nc, ctx, "ps_b", [128, 512])
    ps_c = psb(nc, ctx, "ps_c", [128, 512])
    ps_d = psb(nc, ctx, "ps_d", [128, 512])
    A["ps_a"], A["ps_b"] = ps_a, ps_b


    toks = []
    toks.append(P.dma("scalar", "a", IDT[:, :], ident_d))
    toks.append(P.dma("scalar", "a", ONES[:, :], ones_d))
    toks.append(P.dma("scalar", "a", FL[0:1, :], flags_d))
    for dc in range(16):
        toks.append(P.dma("scalar", "a", BQ[:, dc:dc + 1], bq[dc * 128:(dc + 1) * 128][:, None]))
        toks.append(P.dma("scalar", "a", BKVk[:, dc:dc + 1], bkv[dc * 128:(dc + 1) * 128][:, None]))
        toks.append(P.dma("scalar", "a", BO[:, dc:dc + 1], bo[dc * 128:(dc + 1) * 128][:, None]))
        toks.append(P.dma("scalar", "a", GQ[:, dc:dc + 1], g_q[dc * 128:(dc + 1) * 128][:, None]))
        toks.append(P.dma("scalar", "a", GKV[:, dc:dc + 1], g_kv[dc * 128:(dc + 1) * 128][:, None]))
    toks.append(P.dma("scalar", "a", BKVrow[0:1, :], bkv[None, :]))
    toks.append(P.dma("scalar", "a", GH[:, 0:1], g_head[:, None]))
    aux = W(*toks)

    g2_ve = P.chain("vector", aux, [
        lambda v: v.tensor_tensor(G2[:, 0:1], GH[:, 0:1], GH[:, 0:1], ALU.mult),
        lambda v: v.tensor_scalar(G2[:, 0:1], G2[:, 0:1],
                                  1.0 / math.sqrt(HD), None, ALU.mult)])
    P.wait("tensor", aux)

    xr_ve, xr_se = emit_ln_split(P, A, xT, XH, XL, with_rms2=True)

    slot_rel = {i: {} for i in range(4)}
    strip_pe = {}
    qk_state = {"qt_dma": None, "q2_pe": 0}

    def whl_lists(sidx, fi, tg):
        whi = [A["WH"][sidx][:, dc * 256 + fi * 128: dc * 256 + fi * 128 + 128]
               for dc in range(16)]
        wlo = [A["WL"][sidx][:, dc * 256 + fi * 128: dc * 256 + fi * 128 + 128]
               for dc in range(16)]
        xhi = [XH[:, dc * TOWN + tg * 512: dc * TOWN + tg * 512 + 512] for dc in range(16)]
        xlo = [XL[:, dc * TOWN + tg * 512: dc * TOWN + tg * 512 + 512] for dc in range(16)]
        return whi, wlo, xhi, xlo

    def emit_qk_evac(kind, fc, tg, pslice, mm_done, bias_tile, dst):
        war_qt = qk_state["qt_dma"]
        war_q2 = qk_state["q2_pe"]
        se_evac = P.chain(
            "scalar",
            W(*[war_qt] if war_qt else [], pe=max(mm_done, war_q2)), [
                lambda a: a.activation(Q2T[:, :], pslice, AF.Square,
                                       bias=bias_tile[:, fc:fc + 1]),
                lambda a: a.activation(QT[:, :], pslice, AF.Identity,
                                       bias=bias_tile[:, fc:fc + 1])])
        rel = {"se": se_evac}
        sums_pe = P.chain("tensor", W(se=se_evac), [
            lambda t: t.matmul(ps_c[:, :], ONES[:, :], Q2T[:, :],
                               start=True, stop=True)])
        qk_state["q2_pe"] = sums_pe
        ve0 = P.chain("vector", W(pe=sums_pe), [
            lambda v: v.tensor_scalar(A["ST"][0][:, :], ps_c[:, :], 1.0 / HD,
                                      EPS_RMS, ALU.mult, ALU.add)])
        se_r, ve_r = emit_rsqrt(P, RS[:, :], A["ST"][0][:, :], A["ST"][1][:, :],
                                A["ST"][2][:, :], v_pre=ve0)
        rs_use = RS
        if kind == "k":
            rs_use = RS2
            P.chain("scalar", W(ve=max(ve_r, g2_ve)), [
                lambda a: a.activation(RS2[:, :], RS[:, :], AF.Identity,
                                       scale=G2[:, 0:1])])
        se_rs = P.val("se")
        ve_fin = P.chain("vector", W(se=se_rs, ve=ve_r), [
            lambda v: v.tensor_tensor(QT[:, :], QT[:, :], rs_use[:, :],
                                      ALU.mult)])
        qk_state["qt_dma"] = P.dma(
            "sync", "q", dst[fc * 128:(fc + 1) * 128, tg * 512:(tg + 1) * 512],
            QT[:, :], waits=W(ve=ve_fin))
        return rel

    def emit_fproj(kind, Wsrc, s0, s1, gain, bias_tile, dst):
        for s in range(s0, s1):
            sidx = s % 2
            war = strip_pe.get((kind, s - 2), 0)
            se_w, ve_w = emit_strip_load(P, A, Wsrc, s * 256, 256, gain, sidx, war)
            for fi in range(2):
                fc = (s - s0) * 2 + fi
                for tg in range(2):
                    slot = (fc * 2 + tg) % 4
                    pslice = ps_s[:, slot * 512:(slot + 1) * 512]
                    whi, wlo, xhi, xlo = whl_lists(sidx, fi, tg)
                    mm_done = emit_mm3(P, pslice, whi, wlo, xhi, xlo,
                                       W(se=max(se_w, slot_rel[slot].get("se", 0)),
                                         ve=max(ve_w, xr_ve,
                                                slot_rel[slot].get("ve", 0))))
                    slot_rel[slot] = emit_qk_evac(kind, fc, tg, pslice,
                                                  mm_done, bias_tile, dst)
            strip_pe[(kind, s)] = P.val("pe")

    emit_fproj("q", Wq, 0, 8, GQ, BQ, Qsc.ap())
    emit_fproj("k", Wkv, 0, 8, GKV, BKVk, Kloc.ap())

    vt_state = {"dma": None}
    for s in range(8, 16):
        sidx = s % 2
        war = strip_pe.get(("v", s - 2), 0)
        se_w, ve_w = emit_strip_load(P, A, Wkv, s * 256, 256, GKV, sidx, war)
        for tt in range(8):
            slot = tt % 4
            pslice = ps_s[:, slot * 512:slot * 512 + 256]
            whi = [XH[:, dc * TOWN + tt * 128: dc * TOWN + tt * 128 + 128]
                   for dc in range(16)]
            wlo = [XL[:, dc * TOWN + tt * 128: dc * TOWN + tt * 128 + 128]
                   for dc in range(16)]
            xhi = [A["WH"][sidx][:, dc * 256:(dc + 1) * 256] for dc in range(16)]
            xlo = [A["WL"][sidx][:, dc * 256:(dc + 1) * 256] for dc in range(16)]
            extra = [(ONES[0:1, 0:128], BKVrow[0:1, s * 256:(s + 1) * 256])]
            mm_done = emit_mm3(P, pslice, whi, wlo, xhi, xlo,
                               W(*[vt_state["dma"]] if False else [],
                                 se=max(se_w, slot_rel[slot].get("se", 0)),
                                 ve=max(ve_w, slot_rel[slot].get("ve", 0))),
                               extra=extra)
            war_vt = vt_state["dma"]
            ve_e = P.chain(
                "vector", W(*[war_vt] if war_vt else [], pe=mm_done), [
                    lambda v, pslice=pslice: v.tensor_copy(VT[:, 0:256], pslice)])
            vt_state["dma"] = P.dma(
                "sync", "q",
                Vloc.ap()[tt * 128:(tt + 1) * 128, (s - 8) * 256:(s - 7) * 256],
                VT[:, 0:256], waits=W(ve=ve_e))
            slot_rel[slot] = {"ve": ve_e}
        strip_pe[("v", s)] = P.val("pe")

    fl_ready = emit_kv_publish(
        P, 0,
        [(Ksh.ap()[:, 0:TOWN], Ksh.ap()[:, TOWN:TFULL], Kloc.ap()[:, :]),
         (Vsh.ap()[0:TOWN, :], Vsh.ap()[TOWN:TFULL, :], Vloc.ap()[:, :])],
        FL, PF, Fsh, W(*P.qtokens("q")))

    head_pe = {}
    tr_rel = {0: {}, 1: {}}
    av_ve = 0
    av_dma = None
    exp_se_last = 0
    for h in range(16):
        war = W(pe=head_pe.get(h - 1, 0), fl=fl_ready,
                **(dict([av_dma]) if av_dma else {}))
        ldk = P.dma("scalar", "a", KHb[:, :],
                    Ksh.ap()[h * 128:(h + 1) * 128, :], waits=war)
        ldv = P.dma("scalar", "a",
                    VHb[:, :].rearrange("p (c f) -> p c f", c=16),
                    Vsh.ap()[:, h * 128:(h + 1) * 128]
                    .rearrange("(c p) f -> p c f", p=128))
        ldq = P.dma("scalar", "a", QHb[:, :], Qsc.ap()[h * 128:(h + 1) * 128, :])
        for qt in range(8):
            rel4 = [slot_rel[i] for i in range(4)]
            s_pe = P.chain(
                "tensor",
                W(ldk, ldv, ldq,
                  se=max([r.get("se", 0) for r in rel4]),
                  ve=max([r.get("ve", 0) for r in rel4])),
                [lambda t, qt=qt, kc=kc: t.matmul(
                    ps_s[:, kc * 512:(kc + 1) * 512],
                    QHb[:, qt * 128:(qt + 1) * 128],
                    KHb[:, kc * 512:(kc + 1) * 512],
                    start=True, stop=True) for kc in range(4)])
            ve_m = P.chain("vector", W(pe=s_pe), [
                lambda v: v.reduce_max(NM[:, 0:1], ps_s[:, :], AX.X, negate=True)])
            se_e = P.chain("scalar", W(ve=ve_m), [
                lambda a: a.activation(P32[:, :], ps_s[:, :], AF.Exp,
                                       bias=NM[:, 0:1], accum_out=SM[:, 0:1])])
            exp_se_last = se_e
            for i in range(4):
                slot_rel[i] = {"se": se_e}
            ve_d = P.chain("vector", W(se=se_e), [
                lambda v: v.reciprocal(NM[:, 0:1], SM[:, 0:1]),
                lambda v: v.tensor_tensor(RS[:, 0:1], NM[:, 0:1], SM[:, 0:1], ALU.mult),
                lambda v: v.tensor_scalar(RS[:, 0:1], RS[:, 0:1], -1.0, 2.0,
                                          ALU.mult, ALU.add),
                lambda v: v.tensor_tensor(NM[:, 0:1], NM[:, 0:1], RS[:, 0:1], ALU.mult),
                lambda v: v.tensor_scalar(P32[:, :], P32[:, :], NM[:, 0:1],
                                          None, ALU.mult)])

            for grp in range(4):
                stg = [ps_a, ps_b][grp % 2]
                rel_t = tr_rel[grp % 2]
                pe_t = P.chain(
                    "tensor",
                    W(ve=max(ve_d, rel_t.get("ve", 0)), se=rel_t.get("se", 0)),
                    [lambda t, grp=grp, stg=stg, j=j: t.transpose(
                        stg[:, j * 128:(j + 1) * 128],
                        P32[:, (grp * 4 + j) * 128:(grp * 4 + j) * 128 + 128],
                        IDT[:, :]) for j in range(4)])
                if grp % 2 == 0:
                    tr_rel[grp % 2] = {"se": P.chain("scalar", W(pe=pe_t), [
                        lambda a, grp=grp, stg=stg: a.activation(
                            PT[:, grp * 512:(grp + 1) * 512], stg[:, :],
                            AF.Identity)])}
                else:
                    tr_rel[grp % 2] = {"ve": P.chain("vector", W(pe=pe_t), [
                        lambda v, grp=grp, stg=stg: v.tensor_copy(
                            PT[:, grp * 512:(grp + 1) * 512], stg[:, :])])}
            pt_se = tr_rel[0].get("se", 0)
            pt_ve = tr_rel[1].get("ve", 0)
            av_pe = P.chain(
                "tensor", W(se=pt_se, ve=max(pt_ve, av_ve)),
                [lambda t, kk=0: t.matmul(ps_c[:, 0:128], VHb[:, 0:128],
                                          PT[:, 0:128], start=True, stop=False)]
                + [lambda t, kk=kk: t.matmul(
                    ps_c[:, 0:128], VHb[:, kk * 128:(kk + 1) * 128],
                    PT[:, kk * 128:(kk + 1) * 128],
                    start=False, stop=(kk == 15)) for kk in range(1, 16)])
            av_ve = P.chain(
                "vector", W(*[av_dma] if av_dma else [], pe=av_pe), [
                    lambda v, qt=qt: v.tensor_copy(
                        AVH[:, qt * 128:(qt + 1) * 128], ps_c[:, 0:128])])
        av_dma = P.dma("sync", "q", AVsc.ap()[h * 128:(h + 1) * 128, :],
                       AVH[:, :], waits=W(ve=av_ve))
        head_pe[h] = P.val("pe")

    AVB = QHb
    prev_ve6 = 0
    for dc in range(16):
        ld = P.dma("sync", "q", AVB[:, :], AVsc.ap()[dc * 128:(dc + 1) * 128, :],
                   waits=W(*([av_dma] if dc == 0 else []), ve=prev_ve6,
                           pe=head_pe[15] if dc == 0 else 0))
        se_h = P.chain("scalar", W(*[ld]), [
            lambda a, dc=dc: a.activation(XH[:, dc * TOWN:(dc + 1) * TOWN],
                                          AVB[:, :], AF.Identity)])
        prev_ve6 = P.chain("vector", W(*[ld], se=se_h), [
            lambda v, dc=dc: v.tensor_tensor(
                XL[:, dc * TOWN:(dc + 1) * TOWN], AVB[:, :],
                XH[:, dc * TOWN:(dc + 1) * TOWN], ALU.subtract)])
    avhl_ve = P.val("ve")
    avhl_se = P.val("se")

    out_state = {"dma": None, "ldx": None}
    st_toks = []
    A["_ws_ve"] = A.get("_ws_ve", 0)
    for s in range(8):
        sidx = s % 2
        war = strip_pe.get(("o", s - 2), 0)
        se_w, ve_w = emit_strip_load(P, A, Wo, s * 256, 256, None, sidx, war)
        for fi in range(2):
            fc = s * 2 + fi
            for tg in range(2):
                slot = (fc * 2 + tg) % 4
                pslice = ps_s[:, slot * 512:(slot + 1) * 512]
                whi, wlo, xhi, xlo = whl_lists(sidx, fi, tg)
                mm_done = emit_mm3(P, pslice, whi, wlo, xhi, xlo,
                                   W(se=max(se_w, slot_rel[slot].get("se", 0)),
                                     ve=max(ve_w, avhl_ve,
                                            slot_rel[slot].get("ve", 0))))
                war_o = out_state["dma"]
                se_o = P.chain(
                    "scalar", W(*[war_o] if war_o else [], pe=mm_done), [
                        lambda a, pslice=pslice, fc=fc: a.activation(
                            QT[:, :], pslice, AF.Identity,
                            bias=BO[:, fc:fc + 1])])
                slot_rel[slot] = {"se": se_o}
                ldx = P.dma("sync", "q", A["XB"][fc % 2][:, :],
                            xT[fc * 128:(fc + 1) * 128, tg * 512:(tg + 1) * 512],
                            waits=W(ve=xr_ve if s == 0 and fi == 0 else 0))
                ve_o = P.chain(
                    "vector", W(ldx, *[war_o] if war_o else [], se=se_o), [
                        lambda v, fc=fc: v.tensor_tensor(
                            Q2T[:, :], QT[:, :], A["XB"][fc % 2][:, :],
                            ALU.add)])
                out_state["dma"] = P.dma(
                    "sync", "q",
                    x1T[fc * 128:(fc + 1) * 128, tg * 512:(tg + 1) * 512],
                    Q2T[:, :], waits=W(ve=ve_o))
                st_toks.append(out_state["dma"])
        strip_pe[("o", s)] = P.val("pe")

    P.wait("sync", W(*st_toks))


def build_block0a(dbg=False):
    nc = bass.Bass("TRN2", target_bir_lowering=False)
    io = declare_block0a_io(nc)
    with ExitStack() as ctx:
        P = Prog(nc, ctx, SEMALL, QALL)
        emit_parity_setup(P)
        record_block0a(P, nc, ctx, io)
        P.build_block()
    return nc


def _prep_block0a_inputs(inputs, consts):
    x = np.ascontiguousarray(inputs["L0_attn_x"].transpose(0, 2, 1))
    maps = []
    for c in range(8):
        s, par = c // 2, c % 2
        m = dict(
            xT=np.ascontiguousarray(x[s, :, par * TOWN:(par + 1) * TOWN]),
            Wq=inputs["L0_attn_Wq"], bq=inputs["L0_attn_bq"],
            Wkv=inputs["L0_attn_Wkv"], bkv=inputs["L0_attn_bkv"],
            Wo=inputs["L0_attn_Wo"], bo=inputs["L0_attn_bo"],
            g_q=inputs["L0_attn_g_q"], g_kv=inputs["L0_attn_g_kv"],
            g_head=inputs["L0_attn_g_head"],
            ident=consts["ident"], ones128=consts["ones128"],
            flagvals=consts["flagvals"],
        )
        maps.append({k: np.ascontiguousarray(np.asarray(
            v, dtype=np.int32 if k == "flagvals" else np.float32)) for k, v in m.items()})
    return maps


def run_block0a(inputs, dbg=False):
    consts = host_consts(np.random.randint(1 << 24))
    nc = build_block0a(dbg)
    maps = _prep_block0a_inputs(inputs, consts)
    res = run_bass_kernel_spmd(nc, maps, core_ids=list(range(8)))
    out = np.zeros((4, D, TFULL), dtype=np.float32)
    for c in range(8):
        s, par = c // 2, c % 2
        out[s, :, par * TOWN:(par + 1) * TOWN] = res.results[c]["x1T"]
    out = out.transpose(0, 2, 1)
    if dbg:
        return out, res
    return out




def emit_quant(P, A, dst_bf, src_ap, width, invs_col, wait):
    QS0, QSI = A["QS0"], A["QSI"]
    se = P.chain("scalar", wait, [
        lambda a: a.activation(QS0[:, 0:width], src_ap, AF.Identity,
                               scale=invs_col)])
    ve = P.chain("vector", W(se=se), [
        lambda v: v.tensor_single_scalar(QSI[:, 0:width], QS0[:, 0:width], 0.5,
                                         ALU.is_ge),
        lambda v: v.tensor_single_scalar(QS0[:, 0:width], QS0[:, 0:width], -0.5,
                                         ALU.is_le),
        lambda v: v.tensor_tensor(dst_bf, QSI[:, 0:width], QS0[:, 0:width],
                                  ALU.subtract)])
    return se, ve


def declare_block1b_io(nc, pfx="", shared=None, xT=None, out=None):
    io = {}
    io["xT"] = xT if xT is not None else nc.dram_tensor(pfx + "xT", [D, TOWN], F32, kind="ExternalInput").ap()
    io["w1"] = nc.dram_tensor(pfx + "w1", [4 * D, D], F32, kind="ExternalInput").ap()
    io["b1"] = nc.dram_tensor(pfx + "b1", [4 * D], F32, kind="ExternalInput").ap()
    io["w2"] = nc.dram_tensor(pfx + "w2", [D, 4 * D], F32, kind="ExternalInput").ap()
    io["b2"] = nc.dram_tensor(pfx + "b2", [D], F32, kind="ExternalInput").ap()
    io["ident_d"] = shared["ident_d"] if shared is not None else nc.dram_tensor("ident", [128, 128], F32, kind="ExternalInput").ap()
    io["ones_d"] = shared["ones_d"] if shared is not None else nc.dram_tensor("ones128", [128, 128], F32, kind="ExternalInput").ap()
    io["oT"] = out if out is not None else nc.dram_tensor(pfx + "oT", [D, TOWN], F32, kind="ExternalOutput").ap()
    io["w1qT_sc"] = nc.dram_tensor(pfx + "w1qT_sc", [64, 16, 128, 128], BF16, kind="Internal")
    io["w2qT_sc"] = nc.dram_tensor(pfx + "w2qT_sc", [16, 64, 128, 128], BF16, kind="Internal")
    return io


def record_block1b(P, nc, ctx, io):
    xT = io["xT"]; w1 = io["w1"]; b1 = io["b1"]
    w2 = io["w2"]; b2 = io["b2"]; ident_d = io["ident_d"]
    ones_d = io["ones_d"]; oT = io["oT"]; w1qT_sc = io["w1qT_sc"]
    w2qT_sc = io["w2qT_sc"]
    FF = 4 * D
    NFC = FF // 128
    A = {}
    A["XH"] = XH = sb(nc, ctx, "XH", [128, 16 * TOWN], BF16)
    A["XL"] = XL = sb(nc, ctx, "XL", [128, 16 * TOWN], BF16)
    WT = [sb(nc, ctx, f"WTq{i}", [128, 2048], BF16) for i in range(2)]
    WQB = WT[1]
    W2T = [sb(nc, ctx, f"W2T{i}", [128, FF], BF16) for i in range(2)]
    A["XB"] = [sb(nc, ctx, f"XB{i}", [128, 512], F32) for i in range(2)]
    A["X2"] = [sb(nc, ctx, f"X2{i}", [128, 512], F32) for i in range(2)]
    A["MB"] = [sb(nc, ctx, f"MB{i}", [128, 512], F32) for i in range(1)]
    A["RB"] = [sb(nc, ctx, f"RB{i}", [128, 512], F32) for i in range(1)]
    A["ST"] = [sb(nc, ctx, f"ST{i}", [128, 512], F32) for i in range(4)]
    A["ONES"] = ONES = sb(nc, ctx, "ONES", [128, 128], F32)
    IDT = sb(nc, ctx, "IDT", [128, 128], F32)
    IDB = sb(nc, ctx, "IDB", [128, 128], BF16)
    B1T = sb(nc, ctx, "B1T", [128, NFC], F32)
    B2T = sb(nc, ctx, "B2T", [128, 16], F32)
    ACC = sb(nc, ctx, "ACC", [128, 4], F32)
    SCL = sb(nc, ctx, "SCL", [128, 4], F32)
    HT = sb(nc, ctx, "HT", [128, 256], F32)
    QT = sb(nc, ctx, "QT", [128, 256], F32)
    Q2T = sb(nc, ctx, "Q2T", [128, 256], F32)
    ps_a = psb(nc, ctx, "ps_a", [128, 512])
    ps_b = psb(nc, ctx, "ps_b", [128, 512])
    ps_c = psb(nc, ctx, "ps_c", [128, 512])
    ps_d = psb(nc, ctx, "ps_d", [128, 512])
    ps_t = psb(nc, ctx, "ps_t", [128, 1024], BF16)
    A["ps_a"], A["ps_b"] = ps_a, ps_b
    pctx = ctx.enter_context(ExitStack())


    toks = [P.dma("scalar", "a", IDT[:, :], ident_d),
            P.dma("scalar", "a", ONES[:, :], ones_d)]
    for fc in range(NFC):
        toks.append(P.dma("scalar", "a", B1T[:, fc:fc + 1],
                          b1[fc * 128:(fc + 1) * 128][:, None]))
    for dc in range(16):
        toks.append(P.dma("scalar", "a", B2T[:, dc:dc + 1],
                          b2[dc * 128:(dc + 1) * 128][:, None]))
    aux = W(*toks)
    P.chain("scalar", aux, [
        lambda a: a.activation(IDB[:, :], IDT[:, :], AF.Identity)])
    P.wait("tensor", aux)

    sctx = pctx.enter_context(ExitStack())
    SCB = [sb(nc, sctx, f"SCB{i}", [128, 4096], F32) for i in range(2)]
    scan_srcs = []
    for s in range(32):
        scan_srcs.append((0, [w1[s * 256:s * 256 + 128, :],
                              w1[s * 256 + 128:s * 256 + 256, :]]))
    for s in range(16):
        for sub in range(2):
            scan_srcs.append((1, [w2[s * 128:(s + 1) * 128,
                                     sub * 4096:sub * 4096 + 2048],
                                  w2[s * 128:(s + 1) * 128,
                                     sub * 4096 + 2048:sub * 4096 + 4096]]))
    ve_acc = P.chain("vector", W(), [lambda v: v.memset(ACC[:, :], 0.0)])
    ve_red = [0, 0]
    for i, (col, srcs) in enumerate(scan_srcs):
        b = i % 2
        eng, qn = [("sync", "q"), ("gpsimd", "a")][i % 2]
        lds = [P.dma(eng, qn,
                     SCB[b][:, h * 2048:(h + 1) * 2048]
                     .rearrange("p (c f) -> p c f", c=4),
                     srcs[h].rearrange("p (c f) -> p c f", c=4),
                     waits=W(ve=ve_red[b]))
               for h in range(2)]
        ve_acc = P.chain("vector", W(*lds), [
            lambda v, b=b: v.reduce_sum(A["ST"][0][:, 0:1], SCB[b][:, :],
                                        AX.X, apply_absolute_value=True),
            lambda v, col=col: v.tensor_tensor(
                ACC[:, col:col + 1], ACC[:, col:col + 1],
                A["ST"][0][:, 0:1], ALU.add)])
        ve_red[b] = ve_acc
    pe_tot = P.chain("tensor", W(ve=ve_acc), [
        lambda t: t.matmul(ps_c[:, 0:4], ONES[:, :], ACC[:, 0:4],
                           start=True, stop=True)])
    ve_s = P.chain("vector", W(pe=pe_tot), [
        lambda v: v.tensor_scalar(A["ST"][0][0:1, 0:2], ps_c[0:1, 0:2],
                                  1.0 / (FF * D), 1e-8, ALU.mult, ALU.max),
        lambda v: v.reciprocal(A["ST"][0][0:1, 2:4], A["ST"][0][0:1, 0:2]),
        lambda v: v.tensor_tensor(A["ST"][1][0:1, 0:2], A["ST"][0][0:1, 2:4],
                                  A["ST"][0][0:1, 0:2], ALU.mult),
        lambda v: v.tensor_scalar(A["ST"][1][0:1, 0:2], A["ST"][1][0:1, 0:2],
                                  -1.0, 2.0, ALU.mult, ALU.add),
        lambda v: v.tensor_tensor(A["ST"][0][0:1, 2:4], A["ST"][0][0:1, 2:4],
                                  A["ST"][1][0:1, 0:2], ALU.mult)])
    pe_b = P.chain("tensor", W(ve=ve_s), [
        lambda t: t.matmul(ps_c[:, 8:10], ONES[0:1, :],
                           A["ST"][0][0:1, 2:4], start=True, stop=True)])
    ve_scl = P.chain("vector", W(pe=pe_b), [
        lambda v: v.tensor_copy(SCL[:, 0:2], ps_c[:, 8:10])])

    xr_ve, xr_se = emit_ln_split(P, A, xT, XH, XL, with_rms2=False)

    sctx.close()
    WSQ = [sb(nc, pctx, f"WSQ{i}", [128, 2048], F32) for i in range(2)]
    QS0b = [sb(nc, pctx, f"QS0{i}", [128, 2048], F32) for i in range(2)]
    QSIb = [sb(nc, pctx, f"QSI{i}", [128, 2048], F32) for i in range(2)]
    QB = [WQB, sb(nc, pctx, "QB1", [128, 2048], BF16)]
    qstate = {"se": [0, 0], "vq": [0, 0], "pe": [0, 0], "ev": 0,
              "toks": [], "i": 0}

    def q_and_t(Wsrc, row0, cols0, invs_col, dst_batches):
        i = qstate["i"]; qstate["i"] += 1
        b = i % 2
        eng, qn = [("sync", "q"), ("gpsimd", "a")][i % 2]
        ld = P.dma(eng, qn,
                   WSQ[b][:, :].rearrange("p (c f) -> p c f", c=8),
                   Wsrc[row0:row0 + 128, cols0:cols0 + 2048]
                   .rearrange("p (c f) -> p c f", c=8),
                   waits=W(se=qstate["se"][b]))
        se_q = P.chain("scalar", W(*[ld], ve=max(ve_scl, qstate["vq"][b])), [
            lambda a, b=b: a.activation(QS0b[b][:, :], WSQ[b][:, :],
                                        AF.Identity, scale=invs_col)])
        qstate["se"][b] = se_q
        ve_q = P.chain("vector", W(se=se_q, pe=qstate["pe"][b]), [
            lambda v, b=b: v.tensor_single_scalar(QSIb[b][:, :], QS0b[b][:, :],
                                                  0.5, ALU.is_ge),
            lambda v, b=b: v.tensor_single_scalar(QS0b[b][:, :], QS0b[b][:, :],
                                                  -0.5, ALU.is_le),
            lambda v, b=b: v.tensor_tensor(QB[b][:, :], QSIb[b][:, :],
                                           QS0b[b][:, :], ALU.subtract)])
        qstate["vq"][b] = ve_q
        for half in range(2):
            pe_t = P.chain("tensor", W(ve=max(ve_q, qstate["ev"])), [
                lambda t, j=j, half=half, b=b: t.transpose(
                    ps_t[:, j * 128:(j + 1) * 128],
                    QB[b][:, (half * 8 + j) * 128:(half * 8 + j) * 128 + 128],
                    IDB[:, :]) for j in range(8)])
            qstate["pe"][b] = pe_t
            ve_e = P.chain("vector", W(pe=pe_t,
                                       **(W(*qstate["toks"][-2:]))), [
                lambda v, half=half: v.tensor_copy(
                    WT[0][:, half * 1024:(half + 1) * 1024], ps_t[:, :])])
            qstate["ev"] = ve_e
            qstate["toks"].append(
                P.dma("scalar", "a" if qn == "q" else "q", dst_batches[half],
                      WT[0][:, half * 1024:(half + 1) * 1024]
                      .rearrange("p (c f) -> p c f", c=8),
                      waits=W(ve=ve_e)))

    for fc in range(NFC):
        dst = [w1qT_sc.ap()[fc, 0:8].rearrange("c p f -> p c f"),
               w1qT_sc.ap()[fc, 8:16].rearrange("c p f -> p c f")]
        q_and_t(w1, fc * 128, 0, SCL[:, 0:1], dst)
    for dc in range(16):
        for sub in range(4):
            dst = [w2qT_sc.ap()[dc, sub * 16:sub * 16 + 8]
                   .rearrange("c p f -> p c f"),
                   w2qT_sc.ap()[dc, sub * 16 + 8:sub * 16 + 16]
                   .rearrange("c p f -> p c f")]
            q_and_t(w2, dc * 128, sub * 2048, SCL[:, 1:2], dst)
    wq_done = W(*P.qtokens("q"), *P.qtokens("a"))
    pctx.close()
    HH = sb(nc, ctx, "HH", [128, NFC * 256], BF16)
    HL = sb(nc, ctx, "HL", [128, NFC * 256], BF16)

    TGW = 256
    out_toks = []
    h_rel = {}
    o_rel = {}
    for tg in range(TOWN // TGW):
        o_guard = P.val("pe")
        for fc in range(NFC):
            sidx = fc % 2
            rel = h_rel.get((tg, fc - 2), h_rel.get((tg - 1, 62 + fc), {}))
            ldw = P.dma("sync", "q",
                        WT[sidx][:, :].rearrange("p (c f) -> p c f", c=16),
                        w1qT_sc.ap()[fc].rearrange("c p f -> p c f"),
                        waits=W(pe=rel.get("pe", 0),
                                **(wq_done if fc < 2 and tg == 0 else {})))
            pslice = [ps_a, ps_b][fc % 2][:, 0:TGW]
            whi = [WT[sidx][:, dc * 128:(dc + 1) * 128] for dc in range(16)]
            xhi = [XH[:, dc * TOWN + tg * TGW: dc * TOWN + tg * TGW + TGW]
                   for dc in range(16)]
            xlo = [XL[:, dc * TOWN + tg * TGW: dc * TOWN + tg * TGW + TGW]
                   for dc in range(16)]
            mm_done = emit_mm3(P, pslice, whi, [], xhi, xlo,
                               W(*[ldw], ve=xr_ve, se=rel.get("se", 0)))
            relh = h_rel.get((tg, fc - 1), h_rel.get((tg - 1, 63), {}))
            se_g = P.chain("scalar", W(pe=mm_done, ve=relh.get("veh", 0)), [
                lambda a, pslice=pslice, fc=fc: a.activation(
                    HT[:, 0:TGW], pslice, AF.Gelu, bias=B1T[:, fc:fc + 1])])
            se_h = P.chain("scalar", W(pe=o_guard if fc == 0 else 0), [
                lambda a, fc=fc: a.activation(
                    HH[:, fc * TGW:(fc + 1) * TGW], HT[:, 0:TGW], AF.Identity)])
            ve_h = P.chain("vector", W(se=se_h), [
                lambda v, fc=fc: v.tensor_tensor(
                    HL[:, fc * TGW:(fc + 1) * TGW], HT[:, 0:TGW],
                    HH[:, fc * TGW:(fc + 1) * TGW], ALU.subtract)])
            h_rel[(tg, fc)] = {"se": se_h, "pe": mm_done, "veh": ve_h}
        h_ve = P.val("ve")
        h_se = P.val("se")

        for dc in range(16):
            sidx = dc % 2
            ldw = P.dma("gpsimd", "a",
                        W2T[sidx][:, :].rearrange("p (c f) -> p c f", c=64),
                        w2qT_sc.ap()[dc].rearrange("c p f -> p c f"),
                        waits=W(pe=o_rel.get((tg, dc - 2),
                                             o_rel.get((tg - 1, 14 + dc), {}))
                                .get("pe", 0),
                                **(wq_done if tg == 0 and dc < 2 else {})))
            pslice = [ps_c, ps_d][dc % 2][:, 0:TGW]
            whi = [W2T[sidx][:, fc * 128:(fc + 1) * 128] for fc in range(NFC)]
            xhi = [HH[:, fc * TGW:(fc + 1) * TGW] for fc in range(NFC)]
            xlo = [HL[:, fc * TGW:(fc + 1) * TGW] for fc in range(NFC)]
            rel2 = o_rel.get((tg, dc - 2), o_rel.get((tg - 1, 14 + dc), {}))
            mm_done = emit_mm3(P, pslice, whi, [], xhi, xlo,
                               W(*[ldw], ve=h_ve,
                                 se=max(h_se, rel2.get("se", 0))))
            war_o = o_rel.get((tg, dc - 1),
                              o_rel.get((tg - 1, 15), {})).get("dma")
            se_o = P.chain("scalar",
                           W(*[war_o] if war_o else [], pe=mm_done), [
                lambda a, pslice=pslice, dc=dc: a.activation(
                    QT[:, 0:TGW], pslice, AF.Identity,
                    bias=B2T[:, dc:dc + 1])])
            prev_veo = o_rel.get((tg, dc - 2),
                                 o_rel.get((tg - 1, 14 + dc), {})).get("veo", 0)
            ldx = P.dma("sync", "q", A["XB"][dc % 2][:, 0:TGW],
                        xT[dc * 128:(dc + 1) * 128, tg * TGW:(tg + 1) * TGW],
                        waits=W(ve=max(xr_ve, prev_veo)))
            ve_o = P.chain("vector",
                           W(ldx, *[war_o] if war_o else [], se=se_o), [
                lambda v, dc=dc: v.tensor_tensor(
                    Q2T[:, 0:TGW], QT[:, 0:TGW], A["XB"][dc % 2][:, 0:TGW],
                    ALU.add)])
            tok = P.dma("sync", "q",
                        oT[dc * 128:(dc + 1) * 128, tg * TGW:(tg + 1) * TGW],
                        Q2T[:, 0:TGW], waits=W(ve=ve_o))
            out_toks.append(tok)
            o_rel[(tg, dc)] = {"se": se_o, "pe": mm_done, "dma": tok,
                               "veo": ve_o}

    P.wait("sync", W(*out_toks))


def build_block1b():
    nc = bass.Bass("TRN2", target_bir_lowering=False)
    io = declare_block1b_io(nc)
    with ExitStack() as ctx:
        P = Prog(nc, ctx, SEMALL, QALL)
        record_block1b(P, nc, ctx, io)
        P.build_block()
    return nc


def _prep_block1b_inputs(inputs, consts, x3):
    xf = np.ascontiguousarray(x3.transpose(0, 2, 1))
    maps = []
    for c in range(8):
        s, par = c // 2, c % 2
        m = dict(xT=np.ascontiguousarray(xf[s, :, par * TOWN:(par + 1) * TOWN]),
                 w1=inputs["L1_ffn_w1"], b1=inputs["L1_ffn_b1"],
                 w2=inputs["L1_ffn_w2"], b2=inputs["L1_ffn_b2"],
                 ident=consts["ident"], ones128=consts["ones128"])
        maps.append({k: np.ascontiguousarray(np.asarray(v, np.float32))
                     for k, v in m.items()})
    return maps


def run_block1b(inputs, x3):
    consts = host_consts(np.random.randint(1 << 24))
    nc = build_block1b()
    maps = _prep_block1b_inputs(inputs, consts, x3)
    res = run_bass_kernel_spmd(nc, maps, core_ids=list(range(8)))
    out = np.zeros((4, D, TFULL), dtype=np.float32)
    for c in range(8):
        s, par = c // 2, c % 2
        out[s, :, par * TOWN:(par + 1) * TOWN] = res.results[c]["oT"]
    return out.transpose(0, 2, 1)


def declare_block0b_io(nc, pfx="", shared=None, xT=None, out=None):
    io = {}
    io["xT"] = xT if xT is not None else nc.dram_tensor(pfx + "xT", [D, TOWN], F32, kind="ExternalInput").ap()
    io["rw"] = nc.dram_tensor(pfx + "rw", [8, D], F32, kind="ExternalInput").ap()
    io["wg"] = nc.dram_tensor(pfx + "wg", [8, D, 1408], F32, kind="ExternalInput").ap()
    io["wu"] = nc.dram_tensor(pfx + "wu", [8, D, 1408], F32, kind="ExternalInput").ap()
    io["wd"] = nc.dram_tensor(pfx + "wd", [8, 1408, D], F32, kind="ExternalInput").ap()
    io["swg"] = nc.dram_tensor(pfx + "swg", [D, 5632], F32, kind="ExternalInput").ap()
    io["swu"] = nc.dram_tensor(pfx + "swu", [D, 5632], F32, kind="ExternalInput").ap()
    io["swd"] = nc.dram_tensor(pfx + "swd", [5632, D], F32, kind="ExternalInput").ap()
    io["sgw"] = nc.dram_tensor(pfx + "sgw", [D], F32, kind="ExternalInput").ap()
    io["ident_d"] = shared["ident_d"] if shared is not None else nc.dram_tensor("ident", [128, 128], F32, kind="ExternalInput").ap()
    io["ones_d"] = shared["ones_d"] if shared is not None else nc.dram_tensor("ones128", [128, 128], F32, kind="ExternalInput").ap()
    io["rsel_d"] = nc.dram_tensor(pfx + "rowsel9", [9, 12 * 128], F32, kind="ExternalInput").ap()
    io["oT"] = out if out is not None else nc.dram_tensor(pfx + "oT", [D, TOWN], F32, kind="ExternalOutput").ap()
    return io


def record_block0b(P, nc, ctx, io):
    xT = io["xT"]; rw = io["rw"]; wg = io["wg"]
    wu = io["wu"]; wd = io["wd"]; swg = io["swg"]
    swu = io["swu"]; swd = io["swd"]; sgw = io["sgw"]
    ident_d = io["ident_d"]; ones_d = io["ones_d"]; rsel_d = io["rsel_d"]
    oT = io["oT"]
    NI = 11
    A = {}
    A["XH"] = XH = sb(nc, ctx, "XH", [128, 16 * TOWN], BF16)
    A["XL"] = XL = sb(nc, ctx, "XL", [128, 16 * TOWN], BF16)
    XACC = sb(nc, ctx, "XACC", [128, 16 * 512], F32)
    HHI = sb(nc, ctx, "HHI", [128, NI * 512], BF16)
    HLO = sb(nc, ctx, "HLO", [128, NI * 512], BF16)
    WSb = [sb(nc, ctx, f"WSs{i}", [128, 16 * 128], F32) for i in range(3)]
    WHb = [sb(nc, ctx, f"WHs{i}", [128, 16 * 128], BF16) for i in range(3)]
    WLb = [sb(nc, ctx, f"WLs{i}", [128, 16 * 128], BF16) for i in range(3)]
    A["WS"] = WSb[0]
    CB = sb(nc, ctx, "CB", [128, 512], F32)
    COMB = sb(nc, ctx, "COMB", [128, 9 * 8], F32)
    COMBF = sb(nc, ctx, "COMBF", [9, TOWN], F32)
    RSGH = sb(nc, ctx, "RSGH", [128, 16 * 9], BF16)
    RSGL = sb(nc, ctx, "RSGL", [128, 16 * 9], BF16)
    RWB = XACC[0:8, 0:D]
    RWH = HHI[0:8, 0:D]
    RWL = HLO[0:8, 0:D]
    SGW = sb(nc, ctx, "SGWt", [128, 16], F32)
    RSEL = sb(nc, ctx, "RSEL", [9, 12 * 128], F32)
    A["XB"] = [sb(nc, ctx, f"XB{i}", [128, 512], F32) for i in range(2)]
    A["X2"] = [sb(nc, ctx, f"X2{i}", [128, 512], F32) for i in range(2)]
    A["MB"] = [sb(nc, ctx, f"MB{i}", [128, 512], F32) for i in range(1)]
    A["RB"] = [sb(nc, ctx, f"RB{i}", [128, 512], F32) for i in range(1)]
    A["ST"] = [sb(nc, ctx, f"ST{i}", [128, 512], F32) for i in range(4)]
    A["ONES"] = ONES = sb(nc, ctx, "ONES", [128, 128], F32)
    IDT = sb(nc, ctx, "IDT", [128, 128], F32)
    IDB = sb(nc, ctx, "IDB", [128, 128], BF16)
    TT2 = sb(nc, ctx, "TT2", [128, 512], F32)
    ps_a = psb(nc, ctx, "ps_a", [128, 512])
    ps_b = psb(nc, ctx, "ps_b", [128, 512])
    ps_c = psb(nc, ctx, "ps_c", [128, 512])
    ps_d = psb(nc, ctx, "ps_d", [128, 512])
    ps_t = psb(nc, ctx, "ps_t", [128, 1024], BF16)
    A["ps_a"], A["ps_b"] = ps_a, ps_b


    toks = [P.dma("scalar", "a", IDT[:, :], ident_d),
            P.dma("scalar", "a", ONES[:, :], ones_d),
            P.dma("scalar", "a", RSEL[:, :], rsel_d),
            P.dma("scalar", "a", RWB[:, :], rw)]
    for dc in range(16):
        toks.append(P.dma("scalar", "a", SGW[:, dc:dc + 1],
                          sgw[dc * 128:(dc + 1) * 128][:, None]))
    aux = W(*toks)
    se0 = P.chain("scalar", aux, [
        lambda a: a.activation(IDB[:, :], IDT[:, :], AF.Identity),
        lambda a: a.activation(RWH[:, :], RWB[:, :], AF.Identity)])
    ve0 = P.chain("vector", W(se=se0), [
        lambda v: v.tensor_tensor(RWL[:, :], RWB[:, :], RWH[:, :],
                                  ALU.subtract)])
    rt_ve = 0
    for dc in range(16):
        pe_t = P.chain("tensor", W(ve=max(ve0, rt_ve), se=se0), [
            lambda t, dc=dc: t.transpose(ps_t[:, 0:8],
                                         RWH[:, dc * 128:(dc + 1) * 128],
                                         IDB[0:8, 0:8]),
            lambda t, dc=dc: t.transpose(ps_t[:, 8:16],
                                         RWL[:, dc * 128:(dc + 1) * 128],
                                         IDB[0:8, 0:8])])
        rt_ve = P.chain("vector", W(pe=pe_t), [
            lambda v, dc=dc: v.tensor_copy(RSGH[:, dc * 9:dc * 9 + 8],
                                           ps_t[:, 0:8]),
            lambda v, dc=dc: v.tensor_copy(RSGL[:, dc * 9:dc * 9 + 8],
                                           ps_t[:, 8:16])])
    se_sg = P.chain("scalar", aux, [
        lambda a: a.activation(
            RSGH[:, :].rearrange("p (c n) -> p c n", n=9)[:, :, 8:9],
            SGW[:, :], AF.Identity)])
    ve_sg = P.chain("vector", W(se=se_sg), [
        lambda v: v.scalar_tensor_tensor(
            RSGL[:, :].rearrange("p (c n) -> p c n", n=9)[:, :, 8:9],
            SGW[:, :], 1.0,
            RSGH[:, :].rearrange("p (c n) -> p c n", n=9)[:, :, 8:9],
            ALU.mult, ALU.subtract)])

    xr_ve, xr_se = emit_ln_split(P, A, xT, XH, XL, with_rms2=False)

    comb_ve = {}
    for tt in range(8):
        whi = [XH[:, dc * TOWN + tt * 128: dc * TOWN + tt * 128 + 128]
               for dc in range(16)]
        wlo = [XL[:, dc * TOWN + tt * 128: dc * TOWN + tt * 128 + 128]
               for dc in range(16)]
        xhi = [RSGH[:, dc * 9:(dc + 1) * 9] for dc in range(16)]
        xlo = [RSGL[:, dc * 9:(dc + 1) * 9] for dc in range(16)]
        mm_done = emit_mm3(P, ps_c[:, 0:9], whi, wlo, xhi, xlo,
                           W(ve=max(xr_ve, ve_sg, comb_ve.get(tt - 1, 0)),
                             se=P.val("se")))
        ve_m = P.chain("vector", W(pe=mm_done), [
            lambda v: v.reduce_max(A["ST"][0][:, 0:1], ps_c[:, 0:8], AX.X,
                                   negate=True)])
        se_e = P.chain("scalar", W(ve=ve_m), [
            lambda a: a.activation(A["ST"][1][:, 0:8], ps_c[:, 0:8], AF.Exp,
                                   bias=A["ST"][0][:, 0:1]),
            lambda a, tt=tt: a.activation(COMB[:, tt * 9 + 8:tt * 9 + 9],
                                          ps_c[:, 8:9], AF.Sigmoid)])
        ve_c = P.chain("vector", W(se=se_e), [
            lambda v: v.reduce_max(A["ST"][0][:, 1:2], A["ST"][1][:, 0:8], AX.X),
            lambda v: v.tensor_scalar(A["ST"][2][:, 0:8], A["ST"][1][:, 0:8],
                                      A["ST"][0][:, 1:2], None, ALU.is_ge),
            lambda v: v.scalar_tensor_tensor(A["ST"][2][:, 0:8],
                                             A["ST"][2][:, 0:8], -1e30,
                                             A["ST"][1][:, 0:8],
                                             ALU.mult, ALU.add),
            lambda v: v.reduce_max(A["ST"][0][:, 2:3], A["ST"][2][:, 0:8], AX.X),
            lambda v: v.tensor_scalar(A["ST"][2][:, 0:8], A["ST"][1][:, 0:8],
                                      A["ST"][0][:, 2:3], None, ALU.is_ge),
            lambda v: v.tensor_tensor(A["ST"][0][:, 3:4], A["ST"][0][:, 1:2],
                                      A["ST"][0][:, 2:3], ALU.add),
            lambda v: v.reciprocal(A["ST"][0][:, 4:5], A["ST"][0][:, 3:4]),
            lambda v: v.tensor_tensor(A["ST"][0][:, 5:6], A["ST"][0][:, 4:5],
                                      A["ST"][0][:, 3:4], ALU.mult),
            lambda v: v.tensor_scalar(A["ST"][0][:, 5:6], A["ST"][0][:, 5:6],
                                      -1.0, 2.0, ALU.mult, ALU.add),
            lambda v: v.tensor_tensor(A["ST"][0][:, 4:5], A["ST"][0][:, 4:5],
                                      A["ST"][0][:, 5:6], ALU.mult),
            lambda v: v.tensor_tensor(A["ST"][2][:, 0:8], A["ST"][1][:, 0:8],
                                      A["ST"][2][:, 0:8], ALU.mult),
            lambda v, tt=tt: v.tensor_scalar(COMB[:, tt * 9:tt * 9 + 8],
                                             A["ST"][2][:, 0:8],
                                             A["ST"][0][:, 4:5], None,
                                             ALU.mult)])
        comb_ve[tt] = ve_c
    cf_ve = 0
    for tt in range(8):
        pe_t = P.chain("tensor", W(ve=max(comb_ve[7], cf_ve),
                                   se=P.val("se")), [
            lambda t, tt=tt: t.transpose(ps_d[:, 0:9]
                                         if False else ps_d[0:9, 0:128],
                                         COMB[:, tt * 9:(tt + 1) * 9],
                                         IDT[:, :])])
        cf_ve = P.chain("vector", W(pe=pe_t), [
            lambda v, tt=tt: v.tensor_copy(COMBF[0:9, tt * 128:(tt + 1) * 128],
                                           ps_d[0:9, 0:128])])

    out_toks = []
    bufst = [{"pe": 0, "se": 0, "ve": 0} for _ in range(3)]
    wb = {"i": 0}
    for tg in range(2):
        t0 = tg * 512
        xacc_toks = []
        for dc in range(16):
            xacc_toks.append(P.dma("sync", "q", XACC[:, dc * 512:(dc + 1) * 512],
                                   xT[dc * 128:(dc + 1) * 128, t0:t0 + 512],
                                   waits=W(ve=P.val("ve") if tg else ve0,
                                           se=0 if tg else se0)))
        gu_state = {"se": 0, "pe": 0, "hve": 0}
        cb_ve = 0
        dn_state = {}
        for e in range(12):
            if e < 8:
                gsrc = wg.ap() if False else wg
                g_ap = lambda i0, e=e: wg[e, :, i0:i0 + 128]
                u_ap = lambda i0, e=e: wu[e, :, i0:i0 + 128]
                d_ap = lambda f0, e=e: wd[e, :, f0:f0 + 128]
            else:
                blk = e - 8
                g_ap = lambda i0, blk=blk: swg[:, blk * 1408 + i0: blk * 1408 + i0 + 128]
                u_ap = lambda i0, blk=blk: swu[:, blk * 1408 + i0: blk * 1408 + i0 + 128]
                d_ap = lambda f0, blk=blk: swd[blk * 1408:(blk + 1) * 1408, f0:f0 + 128]

            def emit_h_evac(icp):
                se_sil = P.chain("scalar", W(pe=gu_state["mm_g"],
                                             ve=gu_state["hve"]), [
                    lambda a: a.activation(TT2[:, :], ps_a[:, 0:512], AF.Silu)])
                gu_state["evac_g"] = se_sil
                ve_h = P.chain("vector", W(se=se_sil, pe=gu_state["mm_u"]), [
                    lambda v: v.tensor_tensor(TT2[:, :], TT2[:, :],
                                              ps_b[:, 0:512], ALU.mult)])
                gu_state["evacv_u"] = ve_h
                gu_state["evac_u"] = P.val("se")
                se_hi = P.chain("scalar", W(ve=ve_h), [
                    lambda a, icp=icp: a.activation(
                        HHI[:, icp * 512:(icp + 1) * 512],
                        TT2[:, :], AF.Identity)])
                ve_lo = P.chain("vector", W(se=se_hi), [
                    lambda v, icp=icp: v.tensor_tensor(
                        HLO[:, icp * 512:(icp + 1) * 512], TT2[:, :],
                        HHI[:, icp * 512:(icp + 1) * 512], ALU.subtract)])
                gu_state["hve"] = ve_lo

            pend_ic = None
            for ic in range(NI):
                chunks = []
                for which, src in (("g", g_ap(ic * 128)), ("u", u_ap(ic * 128))):
                    b = wb["i"] % 3
                    eng, qn = [("sync", "q"), ("gpsimd", "a")][wb["i"] % 2]
                    wb["i"] += 1
                    bs = bufst[b]
                    ld = P.dma(eng, qn,
                               WSb[b][:, :].rearrange("p (c f) -> p c f", c=16),
                               src.rearrange("(c p) f -> p c f", p=128),
                               waits=W(se=bs["se"], ve=bs["ve"]))
                    se_w = P.chain("scalar", W(*[ld], pe=bs["pe"]), [
                        lambda a, b=b: a.activation(WHb[b][:, :], WSb[b][:, :],
                                                    AF.Identity)])
                    ve_w = P.chain("vector", W(*[ld], se=se_w), [
                        lambda v, b=b: v.tensor_tensor(WLb[b][:, :], WSb[b][:, :],
                                                       WHb[b][:, :],
                                                       ALU.subtract)])
                    bs["se"], bs["ve"] = se_w, ve_w
                    chunks.append((which, b, bs, se_w, ve_w))
                if pend_ic is not None:
                    emit_h_evac(pend_ic)
                for which, b, bs, se_w, ve_w in chunks:
                    psel = ps_a if which == "g" else ps_b
                    whi = [WHb[b][:, dc * 128:(dc + 1) * 128] for dc in range(16)]
                    wlo = [WLb[b][:, dc * 128:(dc + 1) * 128] for dc in range(16)]
                    xhi = [XH[:, dc * TOWN + t0: dc * TOWN + t0 + 512]
                           for dc in range(16)]
                    xlo = [XL[:, dc * TOWN + t0: dc * TOWN + t0 + 512]
                           for dc in range(16)]
                    mm_done = emit_mm3(P, psel[:, 0:512], whi, wlo, xhi, xlo,
                                       W(se=max(se_w, gu_state.get(f"evac_{which}", 0)),
                                         ve=max(ve_w, xr_ve,
                                                gu_state.get(f"evacv_{which}", 0))))
                    gu_state["pe"] = mm_done
                    bs["pe"] = mm_done
                    gu_state[f"mm_{which}"] = mm_done
                pend_ic = ic
            emit_h_evac(pend_ic)
            h_ve = P.val("ve")
            h_se = P.val("se")

            pe_cb = P.chain("tensor", W(ve=max(cf_ve, P.val("ve")),
                                        se=P.val("se")), [
                lambda t, e=e, t0=t0: t.matmul(ps_d[:, 0:512],
                                               RSEL[:, e * 128:(e + 1) * 128],
                                               COMBF[0:9, t0:t0 + 512],
                                               start=True, stop=True)])
            cb_ve = P.chain("vector", W(pe=pe_cb), [
                lambda v: v.tensor_copy(CB[:, 0:512], ps_d[:, 0:512])])

            def emit_d_evac(pend):
                fcp, pselp, mmp = pend
                ve_o = P.chain("vector", W(pe=mmp), [
                    lambda v, pselp=pselp: v.tensor_tensor(
                        TT2[:, :], pselp[:, 0:512], CB[:, 0:512], ALU.mult),
                    lambda v, fcp=fcp: v.tensor_tensor(
                        XACC[:, fcp * 512:(fcp + 1) * 512],
                        XACC[:, fcp * 512:(fcp + 1) * 512], TT2[:, :],
                        ALU.add)])
                dn_state[fcp % 2] = {"ve": ve_o}

            pend_d = None
            for fc in range(16):
                b = wb["i"] % 3
                eng, qn = [("sync", "q"), ("gpsimd", "a")][wb["i"] % 2]
                wb["i"] += 1
                bs = bufst[b]
                ld = P.dma(eng, qn,
                           WSb[b][:, 0:NI * 128].rearrange("p (c f) -> p c f", c=NI),
                           d_ap(fc * 128).rearrange("(c p) f -> p c f", p=128),
                           waits=W(se=bs["se"], ve=bs["ve"]))
                se_w = P.chain("scalar", W(*[ld], pe=bs["pe"]), [
                    lambda a, b=b: a.activation(WHb[b][:, 0:NI * 128],
                                                WSb[b][:, 0:NI * 128],
                                                AF.Identity)])
                ve_w = P.chain("vector", W(*[ld], se=se_w), [
                    lambda v, b=b: v.tensor_tensor(WLb[b][:, 0:NI * 128],
                                                   WSb[b][:, 0:NI * 128],
                                                   WHb[b][:, 0:NI * 128],
                                                   ALU.subtract)])
                bs["se"], bs["ve"] = se_w, ve_w
                if pend_d is not None:
                    emit_d_evac(pend_d)
                psel = [ps_c, ps_d][fc % 2]
                whi = [WHb[b][:, i * 128:(i + 1) * 128] for i in range(NI)]
                wlo = [WLb[b][:, i * 128:(i + 1) * 128] for i in range(NI)]
                xhi = [HHI[:, i * 512:(i + 1) * 512] for i in range(NI)]
                xlo = [HLO[:, i * 512:(i + 1) * 512] for i in range(NI)]
                prev = dn_state.get(fc % 2, {})
                mm_done = emit_mm3(P, psel[:, 0:512], whi, wlo, xhi, xlo,
                                   W(se=max(se_w, h_se),
                                     ve=max(ve_w, h_ve, prev.get("ve", 0),
                                            cb_ve)))
                gu_state["pe"] = mm_done
                bs["pe"] = mm_done
                pend_d = (fc, psel, mm_done)
            emit_d_evac(pend_d)
            gu_state["hve"] = max(gu_state["hve"], P.val("ve"))
            gu_state["se"] = max(gu_state["se"], P.val("se"))
        for dc in range(16):
            out_toks.append(P.dma("sync", "q",
                                  oT[dc * 128:(dc + 1) * 128, t0:t0 + 512],
                                  XACC[:, dc * 512:(dc + 1) * 512],
                                  waits=W(ve=P.val("ve"))))
    P.wait("sync", W(*out_toks))


def build_block0b():
    nc = bass.Bass("TRN2", target_bir_lowering=False)
    io = declare_block0b_io(nc)
    with ExitStack() as ctx:
        P = Prog(nc, ctx, SEMALL, QALL)
        record_block0b(P, nc, ctx, io)
        P.build_block()
    return nc


def _prep_block0b_inputs(inputs, consts, x1):
    xf = np.ascontiguousarray(x1.transpose(0, 2, 1))
    rsel = np.zeros((9, 12 * 128), dtype=np.float32)
    for e in range(12):
        rsel[min(e, 8), e * 128:(e + 1) * 128] = 1.0
    maps = []
    for c in range(8):
        s, par = c // 2, c % 2
        m = dict(xT=np.ascontiguousarray(xf[s, :, par * TOWN:(par + 1) * TOWN]),
                 rw=inputs["L0_ffn_router_w"], wg=inputs["L0_ffn_w_gate"],
                 wu=inputs["L0_ffn_w_up"], wd=inputs["L0_ffn_w_down"],
                 swg=inputs["L0_ffn_sw_gate"], swu=inputs["L0_ffn_sw_up"],
                 swd=inputs["L0_ffn_sw_down"], sgw=inputs["L0_ffn_shared_gate_w"],
                 ident=consts["ident"], ones128=consts["ones128"], rowsel9=rsel)
        maps.append({k: np.ascontiguousarray(np.asarray(v, np.float32))
                     for k, v in m.items()})
    return maps


def run_block0b(inputs, x1):
    consts = host_consts(np.random.randint(1 << 24))
    nc = build_block0b()
    maps = _prep_block0b_inputs(inputs, consts, x1)
    res = run_bass_kernel_spmd(nc, maps, core_ids=list(range(8)))
    out = np.zeros((4, D, TFULL), dtype=np.float32)
    for c in range(8):
        s, par = c // 2, c % 2
        out[s, :, par * TOWN:(par + 1) * TOWN] = res.results[c]["oT"]
    return out.transpose(0, 2, 1)


def declare_block1a_io(nc, pfx="", shared=None, xT=None, out=None):
    io = {}
    io["xT"] = xT if xT is not None else nc.dram_tensor(pfx + "xT", [D, TOWN], F32, kind="ExternalInput").ap()
    io["Wq"] = nc.dram_tensor(pfx + "Wq", [D, D], F32, kind="ExternalInput").ap()
    io["Wk"] = nc.dram_tensor(pfx + "Wk", [KVD, D], F32, kind="ExternalInput").ap()
    io["Wv"] = nc.dram_tensor(pfx + "Wv", [KVD, D], F32, kind="ExternalInput").ap()
    io["Wo"] = nc.dram_tensor(pfx + "Wo", [D, D], F32, kind="ExternalInput").ap()
    io["Aq"] = nc.dram_tensor(pfx + "Aq", [LORA_R, D], F32, kind="ExternalInput").ap()
    io["Ak"] = nc.dram_tensor(pfx + "Ak", [LORA_R, D], F32, kind="ExternalInput").ap()
    io["Av"] = nc.dram_tensor(pfx + "Av", [LORA_R, D], F32, kind="ExternalInput").ap()
    io["Ao"] = nc.dram_tensor(pfx + "Ao", [LORA_R, D], F32, kind="ExternalInput").ap()
    io["Bq"] = nc.dram_tensor(pfx + "Bq", [D, LORA_R], F32, kind="ExternalInput").ap()
    io["Bk"] = nc.dram_tensor(pfx + "Bk", [KVD, LORA_R], F32, kind="ExternalInput").ap()
    io["Bv"] = nc.dram_tensor(pfx + "Bv", [KVD, LORA_R], F32, kind="ExternalInput").ap()
    io["Bo"] = nc.dram_tensor(pfx + "Bo", [D, LORA_R], F32, kind="ExternalInput").ap()
    io["s4"] = nc.dram_tensor(pfx + "s4", [1, 4], F32, kind="ExternalInput").ap()
    io["qgain"] = nc.dram_tensor(pfx + "qgain", [1, NH], F32, kind="ExternalInput").ap()
    io["cos_d"] = nc.dram_tensor(pfx + "cos_t", [32, TOWN], F32, kind="ExternalInput").ap()
    io["sin_d"] = nc.dram_tensor(pfx + "sin_t", [32, TOWN], F32, kind="ExternalInput").ap()
    io["msk_d"] = nc.dram_tensor(pfx + "msk", [8, 128, TFULL], BF16, kind="ExternalInput").ap()
    io["ident_d"] = shared["ident_d"] if shared is not None else nc.dram_tensor("ident", [128, 128], F32, kind="ExternalInput").ap()
    io["ones_d"] = shared["ones_d"] if shared is not None else nc.dram_tensor("ones128", [128, 128], F32, kind="ExternalInput").ap()
    io["flags_d"] = shared["flags_d"] if shared is not None else nc.dram_tensor("flagvals", [1, 8], I32, kind="ExternalInput").ap()
    io["oT"] = out if out is not None else nc.dram_tensor(pfx + "oT", [D, TOWN], F32, kind="ExternalOutput").ap()
    io["Qsc"] = nc.dram_tensor(pfx + "Qsc", [D, TOWN], F32, kind="Internal")
    io["AVsc"] = nc.dram_tensor(pfx + "AVsc", [D, TOWN], F32, kind="Internal")
    io["Kloc"] = nc.dram_tensor(pfx + "Kloc", [KVD, TOWN], F32, kind="Internal")
    io["Vloc"] = nc.dram_tensor(pfx + "Vloc", [TOWN, KVD], F32, kind="Internal")
    io["Ksh"] = nc.dram_tensor(pfx + "Ksh1", [KVD, TFULL], F32, kind="Internal", addr_space="Shared")
    io["Vsh"] = nc.dram_tensor(pfx + "Vsh1", [TFULL, KVD], F32, kind="Internal", addr_space="Shared")
    io["Fsh"] = nc.dram_tensor(pfx + "Fsh1", [2, 16], I32, kind="Internal", addr_space="Shared")
    return io


def record_block1a(P, nc, ctx, io):
    xT = io["xT"]; Wq = io["Wq"]; Wk = io["Wk"]
    Wv = io["Wv"]; Wo = io["Wo"]; Aq = io["Aq"]
    Ak = io["Ak"]; Av = io["Av"]; Ao = io["Ao"]
    Bq = io["Bq"]; Bk = io["Bk"]; Bv = io["Bv"]
    Bo = io["Bo"]; s4 = io["s4"]; qgain = io["qgain"]
    cos_d = io["cos_d"]; sin_d = io["sin_d"]; msk_d = io["msk_d"]
    ident_d = io["ident_d"]; ones_d = io["ones_d"]; flags_d = io["flags_d"]
    oT = io["oT"]; Qsc = io["Qsc"]; AVsc = io["AVsc"]
    Kloc = io["Kloc"]; Vloc = io["Vloc"]; Ksh = io["Ksh"]
    Vsh = io["Vsh"]; Fsh = io["Fsh"]
    KVD = KVH * HD
    A = {}
    A["XH"] = XH = sb(nc, ctx, "XH", [128, 16 * TOWN], BF16)
    A["XL"] = XL = sb(nc, ctx, "XL", [128, 16 * TOWN], BF16)
    MSKb = [sb(nc, ctx, "MSKb0", [128, TFULL], BF16)]
    A["WS"] = WS = sb(nc, ctx, "WS", [128, 2048], F32)
    WHs = sb(nc, ctx, "WHs", [128, 2048], BF16)
    WLs = sb(nc, ctx, "WLs", [128, 2048], BF16)
    WTH = sb(nc, ctx, "WTH", [128, 2048], BF16)
    WTL = sb(nc, ctx, "WTL", [128, 2048], BF16)
    P32 = sb(nc, ctx, "P32", [128, 2048], F32)
    PT = sb(nc, ctx, "PT", [128, 2048], F32)
    KHb = sb(nc, ctx, "KHb", [128, 2048], F32)
    VHb = sb(nc, ctx, "VHb", [128, 2048], F32)
    QHb = sb(nc, ctx, "QHb", [128, TOWN], F32)
    AVH = sb(nc, ctx, "AVH", [128, TOWN], F32)
    COS = sb(nc, ctx, "COSt", [32, TOWN], F32)
    SIN = sb(nc, ctx, "SINt", [32, TOWN], F32)
    ATH = {k: sb(nc, ctx, f"ATH{k}", [128, 16 * 32], BF16) for k in "qkvo"}
    ATL = {k: sb(nc, ctx, f"ATL{k}", [128, 16 * 32], BF16) for k in "qkvo"}
    BTH = {k: sb(nc, ctx, f"BTH{k}", [32, (16 if k in "qo" else 8) * 128], BF16) for k in "qkvo"}
    BTL = {k: sb(nc, ctx, f"BTL{k}", [32, (16 if k in "qo" else 8) * 128], BF16) for k in "qkvo"}
    ZH = {k: sb(nc, ctx, f"ZH{k}", [32, TOWN], BF16) for k in "qkv"}
    ZL = {k: sb(nc, ctx, f"ZL{k}", [32, TOWN], BF16) for k in "qkv"}
    ZH["o"] = ZH["q"]
    ZL["o"] = ZL["q"]
    A["XB"] = [sb(nc, ctx, f"XB{i}", [128, 512], F32) for i in range(2)]
    A["X2"] = [sb(nc, ctx, f"X2{i}", [128, 512], F32) for i in range(2)]
    A["MB"] = [sb(nc, ctx, "MB0", [128, 512], F32)]
    A["RB"] = [sb(nc, ctx, "RB0", [128, 512], F32)]
    A["ST"] = [sb(nc, ctx, f"ST{i}", [128, 512], F32) for i in range(4)]
    A["ONES"] = ONES = sb(nc, ctx, "ONES", [128, 128], F32)
    IDT = sb(nc, ctx, "IDT", [128, 128], F32)
    IDB = sb(nc, ctx, "IDB", [128, 128], BF16)
    SQB = sb(nc, ctx, "SQB", [128, 4], F32)
    QG = sb(nc, ctx, "QG", [128, NH], F32)
    S1 = sb(nc, ctx, "S1", [1, 4], F32)
    QGr = sb(nc, ctx, "QGr", [1, NH], F32)
    FL = sb(nc, ctx, "FL", [1, 8], I32)
    PF = sb(nc, ctx, "PF", [1, 1], I32)
    NM = sb(nc, ctx, "NM", [128, 1], F32)
    SM = sb(nc, ctx, "SM", [128, 1], F32)
    ps_s = psb(nc, ctx, "ps_s", [128, 2048])
    ps_a = psb(nc, ctx, "ps_a", [128, 512])
    ps_b = psb(nc, ctx, "ps_b", [128, 512])
    ps_c = psb(nc, ctx, "ps_c", [128, 512])
    ps_t = psb(nc, ctx, "ps_t", [128, 1024], BF16)
    A["ps_a"], A["ps_b"] = ps_a, ps_b

    ABUF = P32
    QT = A["X2"][0]
    Q2T = A["X2"][1]
    ABH = WTH
    ABL = WTL
    RS = A["MB"][0]
    Z32 = PT

    toks = [P.dma("scalar", "a", IDT[:, :], ident_d),
            P.dma("scalar", "a", ONES[:, :], ones_d),
            P.dma("scalar", "a", FL[0:1, :], flags_d),
            P.dma("scalar", "a", COS[:, :], cos_d),
            P.dma("scalar", "a", SIN[:, :], sin_d),
            P.dma("scalar", "a", S1[0:1, :], s4),
            P.dma("scalar", "a", QGr[0:1, :], qgain)]

    aux = W(*toks)
    se_id = P.chain("scalar", aux, [
        lambda a: a.activation(IDB[:, :], IDT[:, :], AF.Identity)])
    pe_s = P.chain("tensor", aux, [
        lambda t: t.matmul(ps_c[:, 0:4], ONES[0:1, :], S1[0:1, :],
                           start=True, stop=True),
        lambda t: t.matmul(ps_c[:, 16:16 + NH], ONES[0:1, :], QGr[0:1, :],
                           start=True, stop=True)])
    ve_s = P.chain("vector", W(pe=pe_s), [
        lambda v: v.tensor_copy(SQB[:, :], ps_c[:, 0:4]),
        lambda v: v.tensor_scalar(QG[:, :], ps_c[:, 16:16 + NH],
                                  1.0 / math.sqrt(HD), None, ALU.mult)])

    lstate = {"se": 0, "ve": 0, "pe": 0}

    def prep_AB(key, Asrc, Bsrc, nb):
        ld = P.dma("scalar", "a", ABUF[0:32, :], Asrc,
                   waits=W(se=lstate["se"], ve=lstate["ve"], pe=lstate["pe"]))
        se1 = P.chain("scalar", W(*[ld]), [
            lambda a: a.activation(ABH[0:32, :], ABUF[0:32, :], AF.Identity)])
        ve1 = P.chain("vector", W(*[ld], se=se1), [
            lambda v: v.tensor_tensor(ABL[0:32, :], ABUF[0:32, :], ABH[0:32, :],
                                      ALU.subtract)])
        for dc in range(16):
            pe1 = P.chain("tensor", W(se=se1, ve=max(ve1, lstate["ve"])), [
                lambda t, dc=dc: t.transpose(
                    ps_t[:, 0:32], ABH[0:32, dc * 128:(dc + 1) * 128],
                    IDB[0:32, 0:32]),
                lambda t, dc=dc: t.transpose(
                    ps_t[:, 32:64], ABL[0:32, dc * 128:(dc + 1) * 128],
                    IDB[0:32, 0:32])])
            lstate["ve"] = P.chain("vector", W(pe=pe1), [
                lambda v, dc=dc: v.tensor_copy(ATH[key][:, dc * 32:(dc + 1) * 32],
                                               ps_t[:, 0:32]),
                lambda v, dc=dc: v.tensor_copy(ATL[key][:, dc * 32:(dc + 1) * 32],
                                               ps_t[:, 32:64])])
            lstate["pe"] = pe1
        ldb = P.dma("scalar", "a",
                    ABUF[0:128, 0:nb * 32].rearrange("p (c f) -> p c f", c=nb),
                    Bsrc.rearrange("(c p) f -> p c f", p=128),
                    waits=W(se=P.val("se"), ve=lstate["ve"], pe=lstate["pe"]))
        se2 = P.chain("scalar", W(*[ldb]), [
            lambda a: a.activation(ABH[:, 0:nb * 32],
                                   ABUF[0:128, 0:nb * 32], AF.Identity)])
        ve2 = P.chain("vector", W(*[ldb], se=se2), [
            lambda v: v.tensor_tensor(ABL[:, 0:nb * 32], ABUF[0:128, 0:nb * 32],
                                      ABH[:, 0:nb * 32], ALU.subtract)])
        for c in range(nb):
            pe2 = P.chain("tensor", W(se=se2, ve=max(ve2, lstate["ve"])), [
                lambda t, c=c: t.transpose(
                    ps_t[0:32, 0:128], ABH[:, c * 32:(c + 1) * 32], IDB[:, :]),
                lambda t, c=c: t.transpose(
                    ps_t[0:32, 128:256], ABL[:, c * 32:(c + 1) * 32], IDB[:, :])])
            lstate["ve"] = P.chain("vector", W(pe=pe2), [
                lambda v, c=c: v.tensor_copy(BTH[key][0:32, c * 128:(c + 1) * 128],
                                             ps_t[0:32, 0:128]),
                lambda v, c=c: v.tensor_copy(BTL[key][0:32, c * 128:(c + 1) * 128],
                                             ps_t[0:32, 128:256])])
            lstate["pe"] = pe2
        lstate["se"] = P.val("se")

    prep_AB("q", Aq, Bq, 16)
    prep_AB("k", Ak, Bk, 8)
    prep_AB("v", Av, Bv, 8)
    prep_AB("o", Ao, Bo, 16)

    xr_ve, xr_se = emit_ln_split(P, A, xT, XH, XL, with_rms2=False)

    def emit_z(key, src_h, src_l, ready_ve, ready_se):
        for tg in range(2):
            whi = [ATH[key][:, dc * 32:(dc + 1) * 32] for dc in range(16)]
            wlo = [ATL[key][:, dc * 32:(dc + 1) * 32] for dc in range(16)]
            xhi = [src_h[:, dc * TOWN + tg * 512: dc * TOWN + tg * 512 + 512]
                   for dc in range(16)]
            xlo = [src_l[:, dc * TOWN + tg * 512: dc * TOWN + tg * 512 + 512]
                   for dc in range(16)]
            mm = emit_mm3(P, ps_b[0:32, 0:512], whi, wlo, xhi, xlo,
                          W(ve=max(ready_ve, lstate["ve"], P.val("ve")),
                            se=max(ready_se, P.val("se"))))
            se_z = P.chain("scalar", W(pe=mm, ve=P.val("ve")), [
                lambda a: a.activation(Z32[0:32, 0:512],
                                       ps_b[0:32, 0:512], AF.Identity),
                lambda a, tg=tg, key=key: a.activation(
                    ZH[key][0:32, tg * 512:(tg + 1) * 512],
                    Z32[0:32, 0:512], AF.Identity)])
            P.chain("vector", W(se=se_z), [
                lambda v, tg=tg, key=key: v.tensor_tensor(
                    ZL[key][0:32, tg * 512:(tg + 1) * 512],
                    Z32[0:32, 0:512],
                    ZH[key][0:32, tg * 512:(tg + 1) * 512], ALU.subtract)])

    emit_z("q", XH, XL, xr_ve, xr_se)
    emit_z("k", XH, XL, xr_ve, xr_se)
    emit_z("v", XH, XL, xr_ve, xr_se)
    z_ve = P.val("ve")
    z_se = P.val("se")

    wstate = {"ld_se": 0, "ld_ve": 0, "tr_pe": 0, "ev_ve": 0, "mm_pe": 0}

    def emit_wstrip(Wsrc, f0, scol):
        ld = P.dma("sync", "q", WS[:, :], Wsrc[f0:f0 + 128, :],
                   waits=W(se=wstate["ld_se"], ve=wstate["ld_ve"]))
        se_w = P.chain("scalar", W(*[ld], pe=wstate["tr_pe"]), [
            lambda a: a.activation(WHs[:, :], WS[:, :], AF.Identity,
                                   scale=SQB[:, scol:scol + 1])])
        ve_w = P.chain("vector", W(*[ld], se=se_w, ve=ve_s), [
            lambda v: v.scalar_tensor_tensor(WLs[:, :], WS[:, :],
                                             SQB[:, scol:scol + 1], WHs[:, :],
                                             ALU.mult, ALU.subtract)])
        wstate["ld_se"] = se_w
        wstate["ld_ve"] = ve_w
        for half in range(4):
            srcbuf = WHs if half < 2 else WLs
            dstbuf = WTH if half < 2 else WTL
            hh = half % 2
            pe_t = P.chain("tensor", W(ve=max(ve_w, wstate["ev_ve"]),
                                       se=se_w), [
                lambda t, j=j, hh=hh, srcbuf=srcbuf: t.transpose(
                    ps_t[:, j * 128:(j + 1) * 128],
                    srcbuf[:, (hh * 8 + j) * 128:(hh * 8 + j) * 128 + 128],
                    IDB[:, :]) for j in range(8)])
            wstate["tr_pe"] = pe_t
            wstate["ev_ve"] = P.chain("vector", W(pe=pe_t,
                                                  pe2=0 if True else 0), [
                lambda v, hh=hh, dstbuf=dstbuf: v.tensor_copy(
                    dstbuf[:, hh * 1024:(hh + 1) * 1024], ps_t[:, :])])
        return wstate["ev_ve"]

    def emit_rope(dst_base_ve, tile, tgw):
        pass

    prstate = {"qt_dma": None, "q2_pe": 0}

    def emit_qk_evac1(kind, h, tg, pslice, mm_done, dst):
        war_qt = prstate["qt_dma"]
        war_q2 = prstate["q2_pe"]
        se_evac = P.chain(
            "scalar", W(*[war_qt] if war_qt else [],
                        pe=max(mm_done, war_q2)), [
                lambda a: a.activation(Q2T[:, :], pslice, AF.Square),
                lambda a: a.activation(QT[:, :], pslice, AF.Identity)])
        sums_pe = P.chain("tensor", W(se=se_evac), [
            lambda t: t.matmul(ps_c[:, 0:512], ONES[:, :], Q2T[:, :],
                               start=True, stop=True)])
        prstate["q2_pe"] = sums_pe
        ve0 = P.chain("vector", W(pe=sums_pe), [
            lambda v: v.tensor_scalar(A["ST"][0][:, :], ps_c[:, 0:512],
                                      1.0 / HD, EPS_RMS, ALU.mult, ALU.add)])
        se_r, ve_r = emit_rsqrt(P, RS[:, :], A["ST"][0][:, :],
                                A["ST"][1][:, :], A["ST"][2][:, :], v_pre=ve0)
        rs_use = RS
        if kind == "q":
            rs_use = A["ST"][3]
            P.chain("scalar", W(ve=max(ve_r, ve_s)), [
                lambda a, h=h: a.activation(A["ST"][3][:, :], RS[:, :],
                                            AF.Identity,
                                            scale=QG[:, h:h + 1])])
        se_rs = P.val("se")
        ve_fin = P.chain("vector", W(se=se_rs, ve=ve_r), [
            lambda v, rs_use=rs_use: v.tensor_tensor(QT[:, :], QT[:, :],
                                                     rs_use[:, :], ALU.mult)])
        toff = tg * 512
        ve_rope = P.chain("vector", W(), [
            lambda v: v.tensor_copy(A["ST"][2][0:32, :], QT[32:64, :]),
            lambda v: v.tensor_tensor(A["ST"][0][0:32, :], QT[0:32, :],
                                      COS[:, toff:toff + 512], ALU.mult),
            lambda v: v.tensor_tensor(A["ST"][1][0:32, :], A["ST"][2][0:32, :],
                                      SIN[:, toff:toff + 512], ALU.mult),
            lambda v: v.tensor_tensor(A["ST"][3][0:32, :], QT[0:32, :],
                                      SIN[:, toff:toff + 512], ALU.mult),
            lambda v: v.tensor_tensor(QT[0:32, :], A["ST"][0][0:32, :],
                                      A["ST"][1][0:32, :], ALU.add),
            lambda v: v.tensor_tensor(A["ST"][1][0:32, :], A["ST"][2][0:32, :],
                                      COS[:, toff:toff + 512], ALU.mult),
            lambda v: v.tensor_tensor(QT[32:64, :], A["ST"][1][0:32, :],
                                      A["ST"][3][0:32, :], ALU.subtract)])
        prstate["qt_dma"] = P.dma(
            "sync", "q", dst[h * 128:(h + 1) * 128, tg * 512:(tg + 1) * 512],
            QT[:, :], waits=W(ve=ve_rope))
        return se_evac

    def emit_fproj1(kind, Wsrc, nstrips, scol, Bkey, dst):
        psrel = {0: 0, 1: 0}
        for h in range(nstrips):
            ev_ve = emit_wstrip(Wsrc, h * 128, scol)
            for tg in range(2):
                slot = (h * 2 + tg) % 2
                pslice = [ps_a, ps_b][slot][:, 0:512]
                whi = [WTH[:, dc * 128:(dc + 1) * 128] for dc in range(16)]
                wlo = [WTL[:, dc * 128:(dc + 1) * 128] for dc in range(16)]
                xhi = [XH[:, dc * TOWN + tg * 512: dc * TOWN + tg * 512 + 512]
                       for dc in range(16)]
                xlo = [XL[:, dc * TOWN + tg * 512: dc * TOWN + tg * 512 + 512]
                       for dc in range(16)]
                extra = [(BTH[Bkey][0:32, h * 128:(h + 1) * 128],
                          ZH[Bkey][0:32, tg * 512:(tg + 1) * 512]),
                         (BTH[Bkey][0:32, h * 128:(h + 1) * 128],
                          ZL[Bkey][0:32, tg * 512:(tg + 1) * 512]),
                         (BTL[Bkey][0:32, h * 128:(h + 1) * 128],
                          ZH[Bkey][0:32, tg * 512:(tg + 1) * 512])]
                mm_done = emit_mm3(P, pslice, whi, wlo, xhi, xlo,
                                   W(ve=max(ev_ve, xr_ve, z_ve),
                                     se=max(z_se, psrel[slot])),
                                   extra=extra)
                wstate["mm_pe"] = mm_done
                psrel[slot] = emit_qk_evac1(kind, h, tg, pslice, mm_done, dst)
            wstate["ev_ve"] = max(wstate["ev_ve"], P.val("ve"))

    def emit_wstrip_guard():
        wstate["ev_ve"] = max(wstate["ev_ve"], P.val("ve"))

    emit_fproj1("q", Wq, 16, 0, "q", Qsc.ap())
    emit_fproj1("k", Wk, 8, 1, "k", Kloc.ap())

    vstate = {"dma": None}
    for hs in range(8):
        ev_ve = emit_wstrip(Wv, hs * 128, 2)
        for tt in range(8):
            pslice = ps_a[:, 0:128] if tt % 2 == 0 else ps_b[:, 0:128]
            whi = [XH[:, dc * TOWN + tt * 128: dc * TOWN + tt * 128 + 128]
                   for dc in range(16)]
            wlo = [XL[:, dc * TOWN + tt * 128: dc * TOWN + tt * 128 + 128]
                   for dc in range(16)]
            xhi = [WTH[:, dc * 128:(dc + 1) * 128] for dc in range(16)]
            xlo = [WTL[:, dc * 128:(dc + 1) * 128] for dc in range(16)]
            extra = [(ZH["v"][0:32, tt * 128:(tt + 1) * 128],
                      BTH["v"][0:32, hs * 128:(hs + 1) * 128]),
                     (ZH["v"][0:32, tt * 128:(tt + 1) * 128],
                      BTL["v"][0:32, hs * 128:(hs + 1) * 128]),
                     (ZL["v"][0:32, tt * 128:(tt + 1) * 128],
                      BTH["v"][0:32, hs * 128:(hs + 1) * 128])]
            mm_done = emit_mm3(P, pslice, whi, wlo, xhi, xlo,
                               W(ve=max(ev_ve, xr_ve, z_ve),
                                 se=max(z_se, P.val("se"))), extra=extra)
            wstate["mm_pe"] = mm_done
            war_vt = vstate["dma"]
            ve_e = P.chain("vector",
                           W(*[war_vt] if war_vt else [], pe=mm_done), [
                lambda v, pslice=pslice: v.tensor_copy(QT[:, 0:128], pslice)])
            vstate["dma"] = P.dma(
                "sync", "q",
                Vloc.ap()[tt * 128:(tt + 1) * 128, hs * 128:(hs + 1) * 128],
                QT[:, 0:128], waits=W(ve=ve_e))
        wstate["ev_ve"] = max(wstate["ev_ve"], P.val("ve"))

    fl_ready = emit_kv_publish(
        P, io.get("fidx", 0),
        [(Ksh.ap()[:, 0:TOWN], Ksh.ap()[:, TOWN:TFULL], Kloc.ap()[:, :]),
         (Vsh.ap()[0:TOWN, :], Vsh.ap()[TOWN:TFULL, :], Vloc.ap()[:, :])],
        FL, PF, Fsh, W(*P.qtokens("q")))

    head_pe = {}
    tr_rel = {0: {}, 1: {}}
    av_ve = 0
    av_dma = None
    slot_rel = {"se": 0, "ve": 0}
    prev_s_pe = 0
    prev_av_pe = 0
    msk_pe = [0, 0]
    msk_tok = [None, None]
    for qh in range(NH):
        kvh = qh // 2
        if qh % 2 == 0:
            ldk = P.dma("scalar", "a", KHb[:, :],
                        Ksh.ap()[kvh * 128:(kvh + 1) * 128, :],
                        waits=W(pe=prev_s_pe, fl=fl_ready))
            ldv = P.dma("gpsimd", "a",
                        VHb[:, :].rearrange("p (c f) -> p c f", c=16),
                        Vsh.ap()[:, kvh * 128:(kvh + 1) * 128]
                        .rearrange("(c p) f -> p c f", p=128),
                        waits=W(pe=prev_av_pe, fl=fl_ready))
        ldq = P.dma("scalar", "a", QHb[:, :], Qsc.ap()[qh * 128:(qh + 1) * 128, :],
                    waits=W(pe=prev_s_pe))
        for qt in range(8):
            if msk_tok[0] is None:
                msk_tok[0] = P.dma("scalar", "a", MSKb[0][:, :],
                                   msk_d[qt, :, :])
            ldm = msk_tok[0]
            sops = []
            for kc in range(4):
                sops.append(lambda t, qt=qt, kc=kc: t.matmul(
                    ps_s[:, kc * 512:(kc + 1) * 512],
                    QHb[:, qt * 128:(qt + 1) * 128],
                    KHb[:, kc * 512:(kc + 1) * 512],
                    start=True, stop=False))
                sops.append(lambda t, kc=kc: t.matmul(
                    ps_s[:, kc * 512:(kc + 1) * 512],
                    IDB[:, :],
                    MSKb[0][:, kc * 512:(kc + 1) * 512],
                    start=False, stop=True))
            s_pe = P.chain(
                "tensor", W(ldk, ldv, ldq, ldm, se=slot_rel["se"],
                            ve=slot_rel["ve"]), sops)
            msk_tok[0] = P.dma("scalar", "a", MSKb[0][:, :],
                               msk_d[(qt + 1) % 8, :, :], waits=W(pe=s_pe))
            ve_mask = P.chain("vector", W(pe=s_pe), [
                lambda v: v.reduce_max(NM[:, 0:1], ps_s[:, :], AX.X,
                                       negate=True)])
            se_e = P.chain("scalar", W(ve=ve_mask), [
                lambda a: a.activation(P32[:, :], ps_s[:, :], AF.Exp,
                                       bias=NM[:, 0:1], accum_out=SM[:, 0:1])])
            slot_rel = {"se": se_e, "ve": ve_mask}
            ve_d = P.chain("vector", W(se=se_e), [
                lambda v: v.reciprocal(NM[:, 0:1], SM[:, 0:1]),
                lambda v: v.tensor_tensor(RS[:, 0:1], NM[:, 0:1], SM[:, 0:1],
                                          ALU.mult),
                lambda v: v.tensor_scalar(RS[:, 0:1], RS[:, 0:1], -1.0, 2.0,
                                          ALU.mult, ALU.add),
                lambda v: v.tensor_tensor(NM[:, 0:1], NM[:, 0:1], RS[:, 0:1],
                                          ALU.mult),
                lambda v: v.tensor_scalar(P32[:, :], P32[:, :], NM[:, 0:1],
                                          None, ALU.mult)])
            for grp in range(4):
                stg = [ps_a, ps_b][grp % 2]
                rel_t = tr_rel[grp % 2]
                pe_t = P.chain(
                    "tensor",
                    W(ve=max(ve_d, rel_t.get("ve", 0)), se=rel_t.get("se", 0)),
                    [lambda t, grp=grp, stg=stg, j=j: t.transpose(
                        stg[:, j * 128:(j + 1) * 128],
                        P32[:, (grp * 4 + j) * 128:(grp * 4 + j) * 128 + 128],
                        IDT[:, :]) for j in range(4)])
                if grp % 2 == 0:
                    tr_rel[grp % 2] = {"se": P.chain("scalar", W(pe=pe_t), [
                        lambda a, grp=grp, stg=stg: a.activation(
                            PT[:, grp * 512:(grp + 1) * 512], stg[:, :],
                            AF.Identity)])}
                else:
                    tr_rel[grp % 2] = {"ve": P.chain("vector", W(pe=pe_t), [
                        lambda v, grp=grp, stg=stg: v.tensor_copy(
                            PT[:, grp * 512:(grp + 1) * 512], stg[:, :])])}
            pt_se = tr_rel[0].get("se", 0)
            pt_ve = tr_rel[1].get("ve", 0)
            av_pe = P.chain(
                "tensor", W(se=pt_se, ve=max(pt_ve, av_ve)),
                [lambda t, kk=kk: t.matmul(
                    ps_c[:, 0:128], VHb[:, kk * 128:(kk + 1) * 128],
                    PT[:, kk * 128:(kk + 1) * 128],
                    start=(kk == 0), stop=(kk == 15)) for kk in range(16)])
            av_ve = P.chain(
                "vector", W(*[av_dma] if av_dma else [], pe=av_pe), [
                    lambda v, qt=qt: v.tensor_copy(
                        AVH[:, qt * 128:(qt + 1) * 128], ps_c[:, 0:128])])
        av_dma = P.dma("sync", "q", AVsc.ap()[qh * 128:(qh + 1) * 128, :],
                       AVH[:, :], waits=W(ve=av_ve))
        head_pe[qh] = P.val("pe")
        prev_s_pe = s_pe
        prev_av_pe = av_pe

    AVB = QHb
    prev_ve6 = 0
    for dc in range(16):
        ld = P.dma("sync", "q", AVB[:, :], AVsc.ap()[dc * 128:(dc + 1) * 128, :],
                   waits=W(*([av_dma] if dc == 0 else []), ve=prev_ve6,
                           pe=head_pe[15] if dc == 0 else 0))
        se_h = P.chain("scalar", W(*[ld]), [
            lambda a, dc=dc: a.activation(XH[:, dc * TOWN:(dc + 1) * TOWN],
                                          AVB[:, :], AF.Identity)])
        prev_ve6 = P.chain("vector", W(*[ld], se=se_h), [
            lambda v, dc=dc: v.tensor_tensor(
                XL[:, dc * TOWN:(dc + 1) * TOWN], AVB[:, :],
                XH[:, dc * TOWN:(dc + 1) * TOWN], ALU.subtract)])
    avhl_ve = P.val("ve")
    avhl_se = P.val("se")
    emit_z("o", XH, XL, avhl_ve, avhl_se)
    zo_ve = P.val("ve")
    zo_se = P.val("se")

    out_state = {"dma": None}
    out_toks = []
    for h in range(16):
        ev_ve = emit_wstrip(Wo, h * 128, 3)
        for tg in range(2):
            slot = (h * 2 + tg) % 2
            pslice = [ps_a, ps_b][slot][:, 0:512]
            whi = [WTH[:, dc * 128:(dc + 1) * 128] for dc in range(16)]
            wlo = [WTL[:, dc * 128:(dc + 1) * 128] for dc in range(16)]
            xhi = [XH[:, dc * TOWN + tg * 512: dc * TOWN + tg * 512 + 512]
                   for dc in range(16)]
            xlo = [XL[:, dc * TOWN + tg * 512: dc * TOWN + tg * 512 + 512]
                   for dc in range(16)]
            extra = [(BTH["o"][0:32, h * 128:(h + 1) * 128],
                      ZH["o"][0:32, tg * 512:(tg + 1) * 512]),
                     (BTH["o"][0:32, h * 128:(h + 1) * 128],
                      ZL["o"][0:32, tg * 512:(tg + 1) * 512]),
                     (BTL["o"][0:32, h * 128:(h + 1) * 128],
                      ZH["o"][0:32, tg * 512:(tg + 1) * 512])]
            mm_done = emit_mm3(P, pslice, whi, wlo, xhi, xlo,
                               W(ve=max(ev_ve, avhl_ve, zo_ve),
                                 se=max(zo_se, P.val("se"))), extra=extra)
            war_o = out_state["dma"]
            se_o = P.chain("scalar",
                           W(*[war_o] if war_o else [], pe=mm_done), [
                lambda a, pslice=pslice: a.activation(QT[:, :], pslice,
                                                      AF.Identity)])
            ldx = P.dma("sync", "q", A["XB"][h % 2][:, :],
                        xT[h * 128:(h + 1) * 128, tg * 512:(tg + 1) * 512],
                        waits=W(ve=P.val("ve") if h < 1 else 0))
            ve_o = P.chain("vector",
                           W(ldx, *[war_o] if war_o else [], se=se_o), [
                lambda v, h=h: v.tensor_tensor(Q2T[:, :], QT[:, :],
                                               A["XB"][h % 2][:, :], ALU.add)])
            tok = P.dma("sync", "q",
                        oT[h * 128:(h + 1) * 128, tg * 512:(tg + 1) * 512],
                        Q2T[:, :], waits=W(ve=ve_o))
            out_state["dma"] = tok
            out_toks.append(tok)
        wstate["ev_ve"] = max(wstate["ev_ve"], P.val("ve"))

    P.wait("sync", W(*out_toks))


def build_block1a():
    nc = bass.Bass("TRN2", target_bir_lowering=False)
    io = declare_block1a_io(nc)
    with ExitStack() as ctx:
        P = Prog(nc, ctx, SEMALL, QALL)
        emit_parity_setup(P)
        record_block1a(P, nc, ctx, io)
        P.build_block()
    return nc


def _prep_block1a_inputs(inputs, consts):
    cos, sin = rope_tables_np()
    maps = []
    for c in range(8):
        s, par = c // 2, c % 2
        tok0 = par * TOWN
        msk = np.zeros((8, 128, TFULL), dtype=np.float32)
        q = np.arange(128)[:, None]
        k = np.arange(TFULL)[None, :]
        for qt in range(8):
            msk[qt] = np.where(k <= tok0 + qt * 128 + q, 0.0, MASK_NEG)
        m = dict(Wq=inputs["L1_attn_W_q"], Wk=inputs["L1_attn_W_k"],
                 Wv=inputs["L1_attn_W_v"], Wo=inputs["L1_attn_W_o"],
                 Aq=inputs["L1_attn_A_q"], Ak=inputs["L1_attn_A_k"],
                 Av=inputs["L1_attn_A_v"], Ao=inputs["L1_attn_A_o"],
                 Bq=inputs["L1_attn_B_q"], Bk=inputs["L1_attn_B_k"],
                 Bv=inputs["L1_attn_B_v"], Bo=inputs["L1_attn_B_o"],
                 s4=np.stack([inputs["L1_attn_s_q"], inputs["L1_attn_s_k"],
                              inputs["L1_attn_s_v"], inputs["L1_attn_s_o"]],
                             axis=1).reshape(1, 4),
                 qgain=inputs["L1_attn_q_gain"][None, :],
                 cos_t=np.ascontiguousarray(cos[:, tok0:tok0 + TOWN]),
                 sin_t=np.ascontiguousarray(sin[:, tok0:tok0 + TOWN]),
                 ident=consts["ident"], ones128=consts["ones128"],
                 flagvals=consts["flagvals"])
        mm = {k2: np.ascontiguousarray(np.asarray(
            v, dtype=np.int32 if k2 == "flagvals" else np.float32))
            for k2, v in m.items()}
        mm["msk"] = msk.astype(BF16NP)
        maps.append(mm)
    return maps


def run_block1a(inputs, x2):
    consts = host_consts(np.random.randint(1 << 24))
    nc = build_block1a()
    maps = _prep_block1a_inputs(inputs, consts)
    xf = np.ascontiguousarray(x2.transpose(0, 2, 1))
    for c in range(8):
        s, par = c // 2, c % 2
        maps[c]["xT"] = np.ascontiguousarray(xf[s, :, par * TOWN:(par + 1) * TOWN])
    res = run_bass_kernel_spmd(nc, maps, core_ids=list(range(8)))
    out = np.zeros((4, D, TFULL), dtype=np.float32)
    for c in range(8):
        s, par = c // 2, c % 2
        out[s, :, par * TOWN:(par + 1) * TOWN] = res.results[c]["oT"]
    return out.transpose(0, 2, 1)


def kernel(**inputs):
    return run_fused(inputs)


def build_fused():
    nc = bass.Bass("TRN2", target_bir_lowering=False)
    shared = {
        "ident_d": nc.dram_tensor("ident", [128, 128], F32,
                                  kind="ExternalInput").ap(),
        "ones_d": nc.dram_tensor("ones128", [128, 128], F32,
                                 kind="ExternalInput").ap(),
        "flags_d": nc.dram_tensor("flagvals", [1, 8], I32,
                                  kind="ExternalInput").ap(),
    }
    x1s = nc.dram_tensor("x1s", [D, TOWN], F32, kind="Internal")
    x2s = nc.dram_tensor("x2s", [D, TOWN], F32, kind="Internal")
    x3s = nc.dram_tensor("x3s", [D, TOWN], F32, kind="Internal")
    oT = nc.dram_tensor("oT", [D, TOWN], F32, kind="ExternalOutput").ap()
    ios = [
        declare_block0a_io(nc, "a0_", shared, out=x1s.ap()),
        declare_block0b_io(nc, "b0_", shared, xT=x1s.ap(), out=x2s.ap()),
        declare_block1a_io(nc, "a1_", shared, xT=x2s.ap(), out=x3s.ap()),
        declare_block1b_io(nc, "b1_", shared, xT=x3s.ap(), out=oT),
    ]
    ios[2]["fidx"] = 1
    recs = [record_block0a, record_block0b, record_block1a, record_block1b]
    with ExitStack() as ctx:
        P = Prog(nc, ctx, SEMALL, QALL)
        emit_parity_setup(P)
        for i, (rec, io) in enumerate(zip(recs, ios)):
            if i:
                P.fence()
            with ExitStack() as bctx:
                rec(P, nc, bctx, io)
        P.build_block()
    return nc


def run_fused(inputs):
    consts = host_consts(np.random.randint(1 << 24))
    nc = build_fused()
    zero = np.zeros((4, TFULL, D), np.float32)
    m0a = _prep_block0a_inputs(inputs, consts)
    m0b = _prep_block0b_inputs(inputs, consts, zero)
    m1a = _prep_block1a_inputs(inputs, consts)
    m1b = _prep_block1b_inputs(inputs, consts, zero)
    shared_keys = ("ident", "ones128", "flagvals")
    maps = []
    for c in range(8):
        fm = {"ident": m0a[c]["ident"], "ones128": m0a[c]["ones128"],
              "flagvals": m0a[c]["flagvals"]}
        for pfx, m in (("a0_", m0a[c]), ("b0_", m0b[c]),
                       ("a1_", m1a[c]), ("b1_", m1b[c])):
            for k, v in m.items():
                if k in shared_keys or (k == "xT" and pfx != "a0_"):
                    continue
                fm[pfx + k] = v
        maps.append(fm)
    res = run_bass_kernel_spmd(nc, maps, core_ids=list(range(8)))
    out = np.zeros((4, D, TFULL), dtype=np.float32)
    for c in range(8):
        s, par = c // 2, c % 2
        out[s, :, par * TOWN:(par + 1) * TOWN] = res.results[c]["oT"]
    return np.ascontiguousarray(out.transpose(0, 2, 1))
